# version 04
# baseline (speedup 1.0000x reference)
import contextlib
import numpy as np
import ml_dtypes
import concourse.bass as bass
import concourse.mybir as mybir
from concourse.bass_utils import run_bass_kernel_spmd

F32 = mybir.dt.float32
BF16 = mybir.dt.bfloat16
AF = mybir.ActivationFunctionType
ALU = mybir.AluOpType
AX = mybir.AxisListType

S = 4096
D = 1024
NT = 32
NH = 16
NEG = -1.0e30
EPS = 1e-5
QSCALE = 1.0 / 8.0

FH = 4
NCHUNK = 8
FF = 16384



STREAMS = ("tensor", "vector", "scalar", "gpsimd", "sync")
SEM_OF = {"tensor": "T", "vector": "V", "scalar": "S", "gpsimd": "G"}
DMA_SEM_OF = {"sync": "qY", "gpsimd": "qG", "scalar": "qS"}


class Buf:
    __slots__ = ("name", "w", "r")

    def __init__(self, name):
        self.name = name
        self.w = []
        self.r = []


class Em:
    def __init__(self, nc):
        self.nc = nc
        self.sem_names = ["T", "V", "S", "G", "qY", "qG", "qS"]
        self.sems = {}
        self.cnt = {n: 0 for n in self.sem_names}
        self.last_wait = {s: {} for s in STREAMS}
        self.thunks = None
        self.nops = 0

    def begin_phase(self):
        self.thunks = {s: [] for s in STREAMS}
        for s in STREAMS:
            self.last_wait[s] = dict(self.cnt)

    def _waits_for(self, stream, deps):
        need = {}
        for (sem, val) in deps:
            if stream == "tensor" and sem == "T":
                continue
            if need.get(sem, 0) < val:
                need[sem] = val
        lw = self.last_wait[stream]
        out = []
        for sem, val in need.items():
            if lw.get(sem, 0) >= val:
                continue
            lw[sem] = val
            out.append((sem, val))
        return out

    def op(self, stream, fn, reads=(), writes=(), dma=False):
        deps = []
        for b in reads:
            deps += b.w
        for b in writes:
            deps += b.w
            deps += b.r
        waits = self._waits_for(stream, deps)
        if dma:
            sem, inc = DMA_SEM_OF[stream], 16
        else:
            sem, inc = SEM_OF[stream], 1
        self.cnt[sem] += inc
        token = (sem, self.cnt[sem])
        sems = self.sems

        def thunk(e, waits=waits, fn=fn, sem=sem, inc=inc):
            for (s_, v_) in waits:
                e.wait_ge(sems[s_], v_)
            fn(e).then_inc(sems[sem], inc)

        self.thunks[stream].append(thunk)
        self.nops += 1
        for b in writes:
            b.w = [token]
            b.r = []
        for b in reads:
            b.r.append(token)
        return token

    def wait_only(self, stream, deps):
        waits = self._waits_for(stream, deps)
        if not waits:
            return
        sems = self.sems

        def thunk(e, waits=waits):
            for (s_, v_) in waits:
                e.wait_ge(sems[s_], v_)

        self.thunks[stream].append(thunk)

    def run_phase(self, name):
        for stream, sem in DMA_SEM_OF.items():
            self.wait_only(stream, [(sem, self.cnt[sem])])
        th = self.thunks
        with self.nc.named_scope(name), self.nc.Block(name) as block:
            if th["sync"]:
                @block.sync
                def _(e, th=th):
                    for t in th["sync"]:
                        t(e)
            if th["gpsimd"]:
                @block.gpsimd
                def _(e, th=th):
                    for t in th["gpsimd"]:
                        t(e)
            if th["tensor"]:
                @block.tensor
                def _(e, th=th):
                    for t in th["tensor"]:
                        t(e)
            if th["vector"]:
                @block.vector
                def _(e, th=th):
                    for t in th["vector"]:
                        t(e)
            if th["scalar"]:
                @block.scalar
                def _(e, th=th):
                    for t in th["scalar"]:
                        t(e)
        self.thunks = None



def build_program(upto="all", dbg=False):
    nc = bass.Bass("TRN2", target_bir_lowering=False)
    em = Em(nc)
    KIND = "ExternalOutput" if dbg else "Internal"

    x_h = nc.dram_tensor("x", [S, D], F32, kind="ExternalInput").ap()
    wq_h = nc.dram_tensor("wqT", [D, D], F32, kind="ExternalInput").ap()
    wk_h = nc.dram_tensor("wkT", [D, D], F32, kind="ExternalInput").ap()
    wv_h = nc.dram_tensor("wvT", [D, D], F32, kind="ExternalInput").ap()
    wo_h = nc.dram_tensor("woT", [D, D], F32, kind="ExternalInput").ap()
    s1w_h = nc.dram_tensor("s1w", [FH, NCHUNK, 128, 128], F32,
                           kind="ExternalInput").ap()
    s2w_h = nc.dram_tensor("s2w", [FH, NCHUNK, 4, 128, 128], F32,
                           kind="ExternalInput").ap()
    wc_h = nc.dram_tensor("wcT", [FF, D], F32, kind="ExternalInput").ap()
    mwin_h = nc.dram_tensor("m_win", [128, 192], F32, kind="ExternalInput").ap()
    mwin0_h = nc.dram_tensor("m_win0", [128, 128], F32, kind="ExternalInput").ap()
    mfix_h = nc.dram_tensor("m_fix", [NT, 128, 192], F32, kind="ExternalInput").ap()
    mph_h = nc.dram_tensor("m_ph", [64, 64], F32, kind="ExternalInput").ap()
    ident_h = nc.dram_tensor("ident_c", [128, 128], BF16, kind="ExternalInput").ap()
    y_h = nc.dram_tensor("y", [S, D], F32, kind="ExternalOutput").ap()

    qT_h = nc.dram_tensor("qT_f", [D, S], BF16, kind=KIND).ap()
    kT_h = nc.dram_tensor("kT_f", [D, S], BF16, kind=KIND).ap()
    vT_h = nc.dram_tensor("vT_f", [D, S], BF16, kind=KIND).ap()
    vtok_h = nc.dram_tensor("vtok", [NT, 128, D], BF16, kind=KIND).ap()
    ow_h = nc.dram_tensor("ow", [8, NT, 128, 66], F32, kind=KIND).ap()
    ln3T_h = nc.dram_tensor("ln3T", [D, S], BF16, kind=KIND).ap()
    x2_h = nc.dram_tensor("x2", [S, D], F32, kind=KIND).ap()
    if dbg:
        dbg_ao_h = nc.dram_tensor("dbg_ao", [128, 8, S], BF16,
                                  kind="ExternalOutput").ap()
        dbg_opt_h = nc.dram_tensor("dbg_opt", [66, S], F32,
                                   kind="ExternalOutput").ap()
        dbg_tok_h = nc.dram_tensor("dbg_tok", [128, 66], F32,
                                   kind="ExternalOutput").ap()
        dbg_ml_h = nc.dram_tensor("dbg_ml", [128, 8], F32,
                                  kind="ExternalOutput").ap()
        dbg_acc_h = nc.dram_tensor("dbg_acc", [2, 128, 16, D], F32,
                                   kind="ExternalOutput").ap()

    qT_hv = qT_h.rearrange("(c p) t -> p c t", p=128)
    kT_hv = kT_h.rearrange("(c p) t -> p c t", p=128)
    ln3T_hv = ln3T_h.rearrange("(c p) t -> p c t", p=128)

    stack = contextlib.ExitStack()
    with stack:
        for n in em.sem_names:
            em.sems[n] = stack.enter_context(nc.semaphore("sem_" + n))

        ident = stack.enter_context(nc.sbuf_tensor("ident", [128, 128], BF16))
        ident32 = stack.enter_context(nc.sbuf_tensor("ident32", [128, 128], F32))
        ident_b, ident32_b = Buf("ident"), Buf("ident32")
        st_t = {}
        st_b = {}
        for n in ("sums", "sumsq", "t_m", "t_var", "t_sd", "t_rstd", "t_negmr"):
            st_t[n] = stack.enter_context(nc.sbuf_tensor("st_" + n, [128, 1], F32))
            st_b[n] = Buf(n)
        scr = stack.enter_context(nc.sbuf_tensor("scr", [128, D], BF16))
        scr_b = Buf("scr")

        def ln_stats(src_ap, src_bufs, W=D):
            inv = 1.0 / W
            em.op("scalar",
                  lambda e: e.activation(scr[:, 0:W], src_ap, AF.Square,
                                         accum_out=st_t["sumsq"][:]),
                  reads=src_bufs, writes=[st_b["sumsq"], scr_b])
            em.op("vector",
                  lambda e: e.reduce_sum(st_t["sums"][:], src_ap, axis=AX.X),
                  reads=src_bufs, writes=[st_b["sums"]])
            em.op("vector",
                  lambda e: e.tensor_scalar_mul(st_t["t_m"][:], st_t["sums"][:],
                                                inv),
                  reads=[st_b["sums"]], writes=[st_b["t_m"]])
            em.op("vector",
                  lambda e: e.tensor_scalar(
                      st_t["t_var"][:], st_t["t_m"][:], st_t["t_m"][:], -EPS,
                      op0=ALU.mult, op1=ALU.add),
                  reads=[st_b["t_m"]], writes=[st_b["t_var"]])
            em.op("vector",
                  lambda e: e.scalar_tensor_tensor(
                      st_t["t_var"][:], st_t["sumsq"][:], inv, st_t["t_var"][:],
                      ALU.mult, ALU.subtract),
                  reads=[st_b["sumsq"], st_b["t_var"]], writes=[st_b["t_var"]])
            em.op("scalar",
                  lambda e: e.activation(st_t["t_sd"][:], st_t["t_var"][:],
                                         AF.Sqrt),
                  reads=[st_b["t_var"]], writes=[st_b["t_sd"]])
            em.op("vector",
                  lambda e: e.reciprocal(st_t["t_rstd"][:], st_t["t_sd"][:]),
                  reads=[st_b["t_sd"]], writes=[st_b["t_rstd"]])
            em.op("vector",
                  lambda e: e.scalar_tensor_tensor(
                      st_t["t_negmr"][:], st_t["t_m"][:], -1.0,
                      st_t["t_rstd"][:], ALU.mult, ALU.mult),
                  reads=[st_b["t_m"], st_b["t_rstd"]], writes=[st_b["t_negmr"]])
            return (st_t["t_rstd"][:], st_t["t_negmr"][:],
                    st_b["t_rstd"], st_b["t_negmr"])

        mid = contextlib.ExitStack()
        with mid:
            aoT = mid.enter_context(nc.sbuf_tensor("aoT", [128, 8, S], BF16))
            aoT_b = Buf("aoT")

            em.begin_phase()
            with contextlib.ExitStack() as ph:
                ec = ph.enter_context
                wq_sb = ec(nc.sbuf_tensor("wq_sb", [128, 8, D], BF16))
                wk_sb = ec(nc.sbuf_tensor("wk_sb", [128, 8, D], BF16))
                wv_sb = ec(nc.sbuf_tensor("wv_sb", [128, 8, D], BF16))
                x_t = ec(nc.sbuf_tensor("x_t", [128, 2, D], F32))
                ln1 = ec(nc.sbuf_tensor("ln1", [128, D], BF16))
                ln1T = ec(nc.sbuf_tensor("ln1T", [128, 2, 8, 512], BF16))
                pstage = ec(nc.sbuf_tensor("pstage", [128, 2, 8, 512], BF16))
                vtstage = ec(nc.sbuf_tensor("vtstage", [128, 4, D], BF16))
                ps_tr1 = ec(nc.psum_tensor("ps_tr1", [128, 2, 1024], BF16))
                ps_mm1 = ec(nc.psum_tensor("ps_mm1", [128, 2, 512], F32))
                wq_b, wk_b, wv_b = Buf("wq"), Buf("wk"), Buf("wv")
                x_t_b = [Buf("x_t0"), Buf("x_t1")]
                ln1_b = Buf("ln1")
                ln1T_b = [Buf("l1T0"), Buf("l1T1")]
                pstage_b = [Buf("ps0"), Buf("ps1")]
                vtstage_b = [Buf("vt%d" % i) for i in range(4)]
                ps_tr1_b = [Buf("ptr0"), Buf("ptr1")]
                ps_mm1_b = [Buf("pmm0"), Buf("pmm1")]

                em.op("sync", lambda e: e.dma_start(ident[:], ident_h),
                      writes=[ident_b], dma=True)
                em.op("vector", lambda e: e.tensor_copy(ident32[:], ident[:]),
                      reads=[ident_b], writes=[ident32_b])
                for wsb, wb, wh in ((wq_sb, wq_b, wq_h), (wk_sb, wk_b, wk_h),
                                    (wv_sb, wv_b, wv_h)):
                    src = wh.rearrange("(c p) o -> p c o", p=128)
                    em.op("gpsimd", lambda e, d=wsb, s=src: e.dma_start(d[:], s),
                          writes=[wb], dma=True)

                for G in range(8):
                    gsl = G % 2
                    for ti in range(4):
                        T = G * 4 + ti
                        sl = T % 2
                        xa = x_t[:, sl, :]
                        em.op("sync",
                              lambda e, xa=xa, T=T: e.dma_start(
                                  xa, x_h[T * 128:(T + 1) * 128, :]),
                              writes=[x_t_b[sl]], dma=True)
                        rstd, negmr, rstd_b, negmr_b = ln_stats(xa, [x_t_b[sl]])
                        em.op("scalar",
                              lambda e, xa=xa, r=rstd, m=negmr: e.activation(
                                  ln1[:], xa, AF.Identity, bias=m, scale=r),
                              reads=[x_t_b[sl], rstd_b, negmr_b],
                              writes=[ln1_b])
                        for f in range(8):
                            psl = f % 2
                            em.op("tensor",
                                  lambda e, f=f, psl=psl: e.transpose(
                                      ps_tr1[:, psl, 0:128],
                                      ln1[:, f * 128:(f + 1) * 128], ident[:]),
                                  reads=[ln1_b, ident_b],
                                  writes=[ps_tr1_b[psl]])
                            em.op("vector",
                                  lambda e, f=f, ti=ti, psl=psl, gsl=gsl:
                                  e.tensor_copy(
                                      ln1T[:, gsl, f, ti * 128:(ti + 1) * 128],
                                      ps_tr1[:, psl, 0:128]),
                                  reads=[ps_tr1_b[psl]], writes=[ln1T_b[gsl]])
                    for pi, (wsb, wb, outdst) in enumerate(
                            ((wq_sb, wq_b, qT_hv), (wk_sb, wk_b, kT_hv),
                             (wv_sb, wv_b, None))):
                        sl = pi % 2
                        for og in range(8):
                            psl = og % 2
                            def mmgrp(e, wsb=wsb, og=og, psl=psl, gsl=gsl):
                                for fg in range(8):
                                    ins = e.matmul(
                                        ps_mm1[:, psl, :],
                                        wsb[:, fg, og * 128:(og + 1) * 128],
                                        ln1T[:, gsl, fg, :],
                                        start=(fg == 0), stop=(fg == 7))
                                return ins
                            em.op("tensor", mmgrp, reads=[wb, ln1T_b[gsl]],
                                  writes=[ps_mm1_b[psl]])
                            if pi == 0:
                                em.op("vector",
                                      lambda e, og=og, psl=psl, sl=sl:
                                      e.tensor_scalar_mul(
                                          pstage[:, sl, og, :],
                                          ps_mm1[:, psl, :], QSCALE),
                                      reads=[ps_mm1_b[psl]],
                                      writes=[pstage_b[sl]])
                            else:
                                em.op("vector",
                                      lambda e, og=og, psl=psl, sl=sl:
                                      e.tensor_copy(pstage[:, sl, og, :],
                                                    ps_mm1[:, psl, :]),
                                      reads=[ps_mm1_b[psl]],
                                      writes=[pstage_b[sl]])
                            if pi == 2:
                                for ti in range(4):
                                    T = G * 4 + ti
                                    tsl = (og * 4 + ti) % 2
                                    em.op("tensor",
                                          lambda e, og=og, ti=ti, sl=sl, tsl=tsl:
                                          e.transpose(
                                              ps_tr1[:, tsl, 0:128],
                                              pstage[:, sl, og,
                                                     ti * 128:(ti + 1) * 128],
                                              ident[:]),
                                          reads=[pstage_b[sl], ident_b],
                                          writes=[ps_tr1_b[tsl]])
                                    em.op("vector",
                                          lambda e, og=og, ti=ti, tsl=tsl:
                                          e.tensor_copy(
                                              vtstage[:, ti,
                                                      og * 128:(og + 1) * 128],
                                              ps_tr1[:, tsl, 0:128]),
                                          reads=[ps_tr1_b[tsl]],
                                          writes=[vtstage_b[ti]])
                        if pi < 2:
                            dst = outdst[:, :, G * 512:(G + 1) * 512]
                            em.op("scalar",
                                  lambda e, dst=dst, sl=sl: e.dma_start(
                                      dst, pstage[:, sl, :, :]),
                                  reads=[pstage_b[sl]], dma=True)
                        else:
                            dst = vT_h.rearrange("(c p) t -> p c t", p=128)[
                                :, :, G * 512:(G + 1) * 512]
                            em.op("scalar",
                                  lambda e, dst=dst, sl=sl: e.dma_start(
                                      dst, pstage[:, sl, :, :]),
                                  reads=[pstage_b[sl]], dma=True)
                            for ti in range(4):
                                T = G * 4 + ti
                                em.op("scalar",
                                      lambda e, T=T, ti=ti: e.dma_start(
                                          vtok_h[T], vtstage[:, ti, :]),
                                      reads=[vtstage_b[ti]], dma=True)
            em.run_phase("A1")
            if upto == "a1":
                return nc, em

            em.begin_phase()
            with contextlib.ExitStack() as ph:
                ec = ph.enter_context
                kwin = ec(nc.sbuf_tensor("kwin", [128, 2, 8, 192], BF16))
                qt = ec(nc.sbuf_tensor("qt", [128, 2, 8, 128], BF16))
                vprev = ec(nc.sbuf_tensor("vprev", [64, 2, D], BF16))
                vcur = ec(nc.sbuf_tensor("vcur", [128, 2, D], BF16))
                ksum = ec(nc.sbuf_tensor("ksum", [128, 4, 64], BF16))
                vsum = ec(nc.sbuf_tensor("vsum", [64, 8, 64], BF16))
                ptmp = ec(nc.sbuf_tensor("ptmp", [128, S], BF16))
                s_sb = ec(nc.sbuf_tensor("s_sb", [128, 2, 192], F32))
                p_sb = ec(nc.sbuf_tensor("p_sb", [128, 2, 192], BF16))
                ptsb = ec(nc.sbuf_tensor("ptsb", [128, 2, 256], BF16))
                owst = ec(nc.sbuf_tensor("owst", [128, 2, 8, 66], F32))
                attn = ec(nc.sbuf_tensor("attn", [128, 2, 64], BF16))
                rl = ec(nc.sbuf_tensor("rl", [128, 2, 1], F32))
                fxs = ec(nc.sbuf_tensor("fxs", [128, 2, 2], F32))
                mwin = ec(nc.sbuf_tensor("mwin", [128, 192], F32))
                mwin0 = ec(nc.sbuf_tensor("mwin0", [128, 128], F32))
                mfix = ec(nc.sbuf_tensor("mfix", [128, NT, 192], F32))
                ps_sc = ec(nc.psum_tensor("ps_sc", [128, 2, 512], F32))
                ps_pt = ec(nc.psum_tensor("ps_pt", [128, 2, 1024], BF16))
                ps_av = ec(nc.psum_tensor("ps_av", [128, 2, 512], F32))
                ps_ao = ec(nc.psum_tensor("ps_ao", [64, 2, 1024], BF16))
                kwin_b = [Buf("kw0"), Buf("kw1")]
                qt_b = [Buf("qt0"), Buf("qt1")]
                vprev_b = [Buf("vp0"), Buf("vp1")]
                vcur_b = [Buf("vc0"), Buf("vc1")]
                ksum_b, vsum_b, ptmp_b = Buf("ksum"), Buf("vsum"), Buf("ptmp")
                s_b = [Buf("s0"), Buf("s1")]
                p_b = [Buf("p0"), Buf("p1")]
                ptsb_b = [Buf("pt0"), Buf("pt1")]
                owst_b = [Buf("ow0"), Buf("ow1")]
                attn_b = [Buf("at0"), Buf("at1")]
                rl_b = [Buf("rl0"), Buf("rl1")]
                fxs_b = [Buf("fx0"), Buf("fx1")]
                mwin_b, mwin0_b, mfix_b = Buf("mwin"), Buf("mwin0"), Buf("mfix")
                ps_sc_b = [Buf("psc0"), Buf("psc1")]
                ps_pt_b = [Buf("ppt0"), Buf("ppt1")]
                ps_av_b = [Buf("pav0"), Buf("pav1")]
                ps_ao_b = [Buf("pao0"), Buf("pao1")]

                em.op("sync", lambda e: e.dma_start(mwin[:], mwin_h),
                      writes=[mwin_b], dma=True)
                em.op("sync", lambda e: e.dma_start(mwin0[:], mwin0_h),
                      writes=[mwin0_b], dma=True)
                em.op("sync", lambda e: e.dma_start(
                    mfix[:], mfix_h.rearrange("t p c -> p t c")),
                      writes=[mfix_b], dma=True)
                for h in range(8, 16):
                    base = (h % 2) * 64
                    em.op("sync",
                          lambda e, h=h, base=base: e.dma_start(
                              ptmp[base:base + 64, :],
                              kT_h[h * 64:(h + 1) * 64, :]),
                          writes=[ptmp_b], dma=True)
                    kv = ptmp.ap().rearrange("p (m ph) -> p m ph", ph=64)
                    em.op("vector",
                          lambda e, h=h, base=base, kv=kv: e.tensor_copy(
                              ksum[base:base + 64, (h - 8) // 2, :],
                              kv[base:base + 64, :, 63]),
                          reads=[ptmp_b], writes=[ksum_b])
                for h in range(8, 16):
                    em.op("sync",
                          lambda e, h=h: e.dma_start(
                              ptmp[0:64, :], vT_h[h * 64:(h + 1) * 64, :]),
                          writes=[ptmp_b], dma=True)
                    vv = ptmp.ap().rearrange("p (m ph) -> p m ph", ph=64)
                    em.op("tensor",
                          lambda e, vv=vv: e.transpose(
                              ps_ao[0:64, 0, 0:64], vv[0:64, :, 63],
                              ident[0:64, 0:64]),
                          reads=[ptmp_b, ident_b], writes=[ps_ao_b[0]])
                    em.op("vector",
                          lambda e, h=h: e.tensor_copy(
                              vsum[:, h - 8, :], ps_ao[0:64, 0, 0:64]),
                          reads=[ps_ao_b[0]], writes=[vsum_b])

                for T in range(NT):
                    sl = T % 2
                    c0 = T * 128
                    em.op("sync",
                          lambda e, sl=sl, c0=c0: e.dma_start(
                              qt[:, sl, :, :], qT_hv[:, :, c0:c0 + 128]),
                          writes=[qt_b[sl]], dma=True)
                    if T == 0:
                        em.op("sync",
                              lambda e, sl=sl: e.dma_start(
                                  kwin[:, sl, :, 64:192], kT_hv[:, :, 0:128]),
                              writes=[kwin_b[sl]], dma=True)
                        em.op("sync",
                              lambda e, sl=sl: e.dma_start(
                                  vcur[:, sl, :], vtok_h[0]),
                              writes=[vcur_b[sl]], dma=True)
                    else:
                        em.op("sync",
                              lambda e, sl=sl, c0=c0: e.dma_start(
                                  kwin[:, sl, :, :],
                                  kT_hv[:, :, c0 - 64:c0 + 128]),
                              writes=[kwin_b[sl]], dma=True)
                        em.op("sync",
                              lambda e, sl=sl, T=T: e.dma_start(
                                  vprev[:, sl, :], vtok_h[T - 1, 64:128, :]),
                              writes=[vprev_b[sl]], dma=True)
                        em.op("sync",
                              lambda e, sl=sl, T=T: e.dma_start(
                                  vcur[:, sl, :], vtok_h[T]),
                              writes=[vcur_b[sl]], dma=True)

                    for h in range(NH):
                        asl = (T * NH + h) % 2
                        base = (h % 2) * 64
                        ch = h // 2
                        strided = h < 8
                        qsl = qt[base:base + 64, sl, ch, :]
                        if strided:
                            if T == 0:
                                em.op("tensor",
                                      lambda e, asl=asl, sl=sl, qsl=qsl,
                                      base=base, ch=ch:
                                      e.matmul(ps_sc[:, asl, 64:192], qsl,
                                               kwin[base:base + 64, sl, ch,
                                                    64:192],
                                               start=True, stop=True),
                                      reads=[qt_b[sl], kwin_b[sl]],
                                      writes=[ps_sc_b[asl]])
                                em.op("vector",
                                      lambda e, asl=asl: e.tensor_add(
                                          s_sb[:, asl, 64:192],
                                          ps_sc[:, asl, 64:192], mwin0[:]),
                                      reads=[ps_sc_b[asl], mwin0_b],
                                      writes=[s_b[asl]])
                                r0, r1 = 64, 192
                            else:
                                em.op("tensor",
                                      lambda e, asl=asl, sl=sl, qsl=qsl,
                                      base=base, ch=ch:
                                      e.matmul(ps_sc[:, asl, 0:192], qsl,
                                               kwin[base:base + 64, sl, ch, :],
                                               start=True, stop=True),
                                      reads=[qt_b[sl], kwin_b[sl]],
                                      writes=[ps_sc_b[asl]])
                                em.op("vector",
                                      lambda e, asl=asl: e.tensor_add(
                                          s_sb[:, asl, 0:192],
                                          ps_sc[:, asl, 0:192], mwin[:]),
                                      reads=[ps_sc_b[asl], mwin_b],
                                      writes=[s_b[asl]])
                                r0, r1 = 0, 192
                            nm_ap = owst[:, sl, h, 64:65]
                            l_ap = owst[:, sl, h, 65:66]
                            nm_wb = owst_b[sl]
                            lbuf = owst_b[sl]
                        else:
                            def scmm(e, asl=asl, sl=sl, qsl=qsl, base=base,
                                     ch=ch, h=h):
                                e.matmul(ps_sc[:, asl, 0:128], qsl,
                                         kwin[base:base + 64, sl, ch, 64:192],
                                         start=True, stop=True)
                                return e.matmul(
                                    ps_sc[:, asl, 128:192], qsl,
                                    ksum[base:base + 64, (h - 8) // 2, :],
                                    start=True, stop=True)
                            em.op("tensor", scmm,
                                  reads=[qt_b[sl], kwin_b[sl], ksum_b],
                                  writes=[ps_sc_b[asl]])
                            em.op("vector",
                                  lambda e, asl=asl, T=T: e.tensor_add(
                                      s_sb[:, asl, 0:192],
                                      ps_sc[:, asl, 0:192], mfix[:, T, :]),
                                  reads=[ps_sc_b[asl], mfix_b],
                                  writes=[s_b[asl]])
                            r0, r1 = 0, 192
                            nm_ap = fxs[:, asl, 0:1]
                            l_ap = fxs[:, asl, 1:2]
                            nm_wb = fxs_b[asl]
                            lbuf = fxs_b[asl]
                        em.op("vector",
                              lambda e, asl=asl, r0=r0, r1=r1, nm=nm_ap:
                              e.reduce_max(nm, s_sb[:, asl, r0:r1], axis=AX.X,
                                           negate=True),
                              reads=[s_b[asl]], writes=[nm_wb])
                        em.op("scalar",
                              lambda e, asl=asl, r0=r0, r1=r1, nm=nm_ap,
                              l=l_ap:
                              e.activation(p_sb[:, asl, r0:r1],
                                           s_sb[:, asl, r0:r1], AF.Exp,
                                           bias=nm, accum_out=l),
                              reads=[s_b[asl], nm_wb], writes=[p_b[asl], lbuf])
                        if strided:
                            def ptr(e, asl=asl, T=T):
                                if T > 0:
                                    e.transpose(ps_pt[0:64, asl, 0:128],
                                                p_sb[:, asl, 0:64], ident[:])
                                return e.transpose(ps_pt[:, asl, 128:256],
                                                   p_sb[:, asl, 64:192],
                                                   ident[:])
                        else:
                            def ptr(e, asl=asl):
                                e.transpose(ps_pt[:, asl, 0:128],
                                            p_sb[:, asl, 0:128], ident[:])
                                return e.transpose(ps_pt[0:64, asl, 128:256],
                                                   p_sb[:, asl, 128:192],
                                                   ident[:])
                        em.op("tensor", ptr, reads=[p_b[asl], ident_b],
                              writes=[ps_pt_b[asl]])
                        em.op("vector",
                              lambda e, asl=asl: e.tensor_copy(
                                  ptsb[:, asl, :], ps_pt[:, asl, 0:256]),
                              reads=[ps_pt_b[asl]], writes=[ptsb_b[asl]])
                        if strided:
                            def avmm(e, asl=asl, sl=sl, h=h, T=T):
                                ins = e.matmul(
                                    ps_av[:, asl, 0:64],
                                    ptsb[0:128, asl, 128:256],
                                    vcur[:, sl, h * 64:(h + 1) * 64],
                                    start=True, stop=(T == 0))
                                if T > 0:
                                    ins = e.matmul(
                                        ps_av[:, asl, 0:64],
                                        ptsb[0:64, asl, 0:128],
                                        vprev[:, sl, h * 64:(h + 1) * 64],
                                        start=False, stop=True)
                                return ins
                            em.op("tensor", avmm,
                                  reads=[ptsb_b[asl], vcur_b[sl]] +
                                        ([vprev_b[sl]] if T > 0 else []),
                                  writes=[ps_av_b[asl]])
                            em.op("vector",
                                  lambda e, asl=asl, sl=sl, h=h: e.tensor_copy(
                                      owst[:, sl, h, 0:64],
                                      ps_av[:, asl, 0:64]),
                                  reads=[ps_av_b[asl]], writes=[owst_b[sl]])
                        else:
                            def avmm(e, asl=asl, sl=sl, h=h):
                                e.matmul(ps_av[:, asl, 0:64],
                                         ptsb[0:128, asl, 0:128],
                                         vcur[:, sl, h * 64:(h + 1) * 64],
                                         start=True, stop=False)
                                return e.matmul(ps_av[:, asl, 0:64],
                                                ptsb[0:64, asl, 128:256],
                                                vsum[:, h - 8, :],
                                                start=False, stop=True)
                            em.op("tensor", avmm,
                                  reads=[ptsb_b[asl], vcur_b[sl], vsum_b],
                                  writes=[ps_av_b[asl]])
                            em.op("vector",
                                  lambda e, asl=asl, l=l_ap: e.reciprocal(
                                      rl[:, asl, :], l),
                                  reads=[lbuf], writes=[rl_b[asl]])
                            em.op("vector",
                                  lambda e, asl=asl: e.tensor_scalar_mul(
                                      attn[:, asl, :], ps_av[:, asl, 0:64],
                                      rl[:, asl, :]),
                                  reads=[ps_av_b[asl], rl_b[asl]],
                                  writes=[attn_b[asl]])
                            em.op("tensor",
                                  lambda e, asl=asl: e.transpose(
                                      ps_ao[:, asl, 0:128], attn[:, asl, :],
                                      ident[:]),
                                  reads=[attn_b[asl], ident_b],
                                  writes=[ps_ao_b[asl]])
                            em.op("vector",
                                  lambda e, asl=asl, base=base, ch=ch, c0=c0:
                                  e.tensor_copy(
                                      aoT[base:base + 64, ch, c0:c0 + 128],
                                      ps_ao[:, asl, 0:128]),
                                  reads=[ps_ao_b[asl]], writes=[aoT_b])
                    dst = ow_h.rearrange("h t p c -> t p h c")[T]
                    em.op("scalar",
                          lambda e, sl=sl, dst=dst: e.dma_start(
                              dst, owst[:, sl, :, :]),
                          reads=[owst_b[sl]], dma=True)
                if dbg:
                    em.op("scalar", lambda e: e.dma_start(dbg_ao_h, aoT[:]),
                          reads=[aoT_b], dma=True)
            em.run_phase("A2a")
            if upto == "a2a":
                return nc, em

            em.begin_phase()
            with contextlib.ExitStack() as ph:
                ec = ph.enter_context
                sq = ec(nc.sbuf_tensor("sq", [64, 2, S], BF16))
                sk = ec(nc.sbuf_tensor("sk", [64, 2, S], BF16))
                sv = ec(nc.sbuf_tensor("sv", [64, 2, S], BF16))
                owh = ec(nc.sbuf_tensor("owh", [128, 2, NT, 66], F32))
                opt = ec(nc.sbuf_tensor("opt", [66, S], F32))
                st2 = ec(nc.sbuf_tensor("st2", [64, 2, 2], F32))
                sph = ec(nc.sbuf_tensor("sph", [64, 2, 64], F32))
                pph = ec(nc.sbuf_tensor("pph", [64, 2, 64], BF16))
                ptvp = ec(nc.sbuf_tensor("ptvp", [64, 2, 128], BF16))
                mph = ec(nc.sbuf_tensor("mph", [64, 64], F32))
                mg_ab = ec(nc.sbuf_tensor("mg_ab", [128, 2, 2], F32))
                mg_d = ec(nc.sbuf_tensor("mg_d", [128, 2, 2], F32))
                mg_nm = ec(nc.sbuf_tensor("mg_nm", [128, 2, 1], F32))
                mg_l = ec(nc.sbuf_tensor("mg_l", [128, 2, 1], F32))
                rl2 = ec(nc.sbuf_tensor("rl2", [128, 2, 1], F32))
                mg_t = ec(nc.sbuf_tensor("mg_t", [128, 2, 64], F32))
                mg_o = ec(nc.sbuf_tensor("mg_o", [128, 2, 64], F32))
                attnb = ec(nc.sbuf_tensor("attnb", [128, 2, 64], BF16))
                tok_sb = ec(nc.sbuf_tensor("tok_sb", [128, 2, 66], F32))
                ps_ph = ec(nc.psum_tensor("ps_ph", [64, 2, 512], F32))
                ps_sm = ec(nc.psum_tensor("ps_sm", [64, 2, 1024], BF16))
                ps_opt = ec(nc.psum_tensor("ps_opt", [66, 2, 512], F32))
                ps_tok = ec(nc.psum_tensor("ps_tok", [64, 512], F32))
                ps_aob = ec(nc.psum_tensor("ps_aob", [64, 1024], BF16))
                sq_b = [Buf("sq0"), Buf("sq1")]
                sk_b = [Buf("sk0"), Buf("sk1")]
                sv_b = [Buf("sv0"), Buf("sv1")]
                owh_b = [Buf("oh0"), Buf("oh1")]
                opt_b = Buf("opt")
                st2_b = [Buf("st20"), Buf("st21")]
                sph_b = [Buf("sp0"), Buf("sp1")]
                pph_b = [Buf("pp0"), Buf("pp1")]
                ptvp_b = [Buf("pv0"), Buf("pv1")]
                mph_b = Buf("mph")
                ab_b = [Buf("ab0"), Buf("ab1")]
                d_b = [Buf("d0"), Buf("d1")]
                nm_b = [Buf("nm0"), Buf("nm1")]
                l_b = [Buf("l0"), Buf("l1")]
                rl2_b = [Buf("rlb0"), Buf("rlb1")]
                t_b = [Buf("t0"), Buf("t1")]
                o_b = [Buf("o0"), Buf("o1")]
                attnb_b = [Buf("atb0"), Buf("atb1")]
                tok_sb_b = [Buf("tk0"), Buf("tk1")]
                ps_ph_b = [Buf("pph0"), Buf("pph1")]
                ps_sm_b = [Buf("psm0"), Buf("psm1")]
                ps_opt_b = [Buf("po0"), Buf("po1")]
                ps_tok_b = Buf("ptk")
                ps_aob_b = Buf("paob")

                em.op("sync", lambda e: e.dma_start(mph[:], mph_h),
                      writes=[mph_b], dma=True)
                for h in range(8):
                    sl = h % 2
                    em.op("sync", lambda e, h=h, sl=sl: e.dma_start(
                        sq[:, sl, :], qT_h[h * 64:(h + 1) * 64, :]),
                          writes=[sq_b[sl]], dma=True)
                    em.op("sync", lambda e, h=h, sl=sl: e.dma_start(
                        sk[:, sl, :], kT_h[h * 64:(h + 1) * 64, :]),
                          writes=[sk_b[sl]], dma=True)
                    em.op("sync", lambda e, h=h, sl=sl: e.dma_start(
                        sv[:, sl, :], vT_h[h * 64:(h + 1) * 64, :]),
                          writes=[sv_b[sl]], dma=True)
                    for T in range(NT):
                        em.op("sync", lambda e, h=h, sl=sl, T=T: e.dma_start(
                            owh[:, sl, T, :], ow_h[h, T]),
                              writes=[owh_b[sl]], dma=True)
                    qv = sq[:, sl, :].rearrange("d (b ph) -> d ph b", ph=64)
                    kv = sk[:, sl, :].rearrange("d (b ph) -> d ph b", ph=64)
                    vv = sv[:, sl, :].rearrange("d (b ph) -> d ph b", ph=64)
                    for p in range(64):
                        psl = p % 2
                        em.op("tensor",
                              lambda e, psl=psl, p=p, qv=qv, kv=kv: e.matmul(
                                  ps_ph[:, psl, 0:64], qv[:, p, :],
                                  kv[:, p, :], start=True, stop=True),
                              reads=[sq_b[sl], sk_b[sl]],
                              writes=[ps_ph_b[psl]])
                        em.op("vector",
                              lambda e, psl=psl: e.tensor_add(
                                  sph[:, psl, :], ps_ph[:, psl, 0:64],
                                  mph[:]),
                              reads=[ps_ph_b[psl], mph_b],
                              writes=[sph_b[psl]])
                        em.op("vector",
                              lambda e, psl=psl: e.reduce_max(
                                  st2[:, psl, 0:1], sph[:, psl, :], axis=AX.X,
                                  negate=True),
                              reads=[sph_b[psl]], writes=[st2_b[psl]])
                        em.op("scalar",
                              lambda e, psl=psl: e.activation(
                                  pph[:, psl, :], sph[:, psl, :], AF.Exp,
                                  bias=st2[:, psl, 0:1],
                                  accum_out=st2[:, psl, 1:2]),
                              reads=[sph_b[psl], st2_b[psl]],
                              writes=[pph_b[psl], st2_b[psl]])
                        def smtr(e, psl=psl, p=p, vv=vv):
                            e.transpose(ps_sm[:, psl, 0:64], pph[:, psl, :],
                                        ident[0:64, 0:64])
                            return e.transpose(ps_sm[:, psl, 64:128],
                                               vv[:, p, :], ident[0:64, 0:64])
                        em.op("tensor", smtr,
                              reads=[pph_b[psl], sv_b[sl], ident_b],
                              writes=[ps_sm_b[psl]])
                        em.op("vector",
                              lambda e, psl=psl: e.tensor_copy(
                                  ptvp[:, psl, :], ps_sm[:, psl, 0:128]),
                              reads=[ps_sm_b[psl]], writes=[ptvp_b[psl]])
                        def optmm(e, psl=psl):
                            e.matmul(ps_opt[0:64, psl, 0:64],
                                     ptvp[:, psl, 64:128], ptvp[:, psl, 0:64],
                                     start=True, stop=True)
                            return e.transpose(ps_opt[0:2, psl, 64:128],
                                               st2[:, psl, :],
                                               ident32[0:64, 0:64])
                        em.op("tensor", optmm,
                              reads=[ptvp_b[psl], st2_b[psl], ident32_b],
                              writes=[ps_opt_b[psl]])
                        def opt_evac(e, psl=psl, p=p):
                            e.tensor_copy(opt[0:64, p * 64:(p + 1) * 64],
                                          ps_opt[0:64, psl, 0:64])
                            return e.tensor_copy(
                                opt[64:66, p * 64:(p + 1) * 64],
                                ps_opt[0:2, psl, 64:128])
                        em.op("vector", opt_evac,
                              reads=[ps_opt_b[psl]], writes=[opt_b])
                    optv = opt.ap().rearrange("r (p bb) -> r p bb", bb=64)
                    for T in range(NT):
                        tsl = T % 2
                        def toktr(e, T=T, optv=optv):
                            e.transpose(ps_tok[:, 0:66], optv[:, :, 2 * T],
                                        ident32[0:66, 0:66])
                            return e.transpose(ps_tok[:, 128:194],
                                               optv[:, :, 2 * T + 1],
                                               ident32[0:66, 0:66])
                        em.op("tensor", toktr, reads=[opt_b, ident32_b],
                              writes=[ps_tok_b])
                        def tokas(e, tsl=tsl):
                            e.tensor_copy(tok_sb[0:64, tsl, :],
                                          ps_tok[:, 0:66])
                            return e.tensor_copy(tok_sb[64:128, tsl, :],
                                                 ps_tok[:, 128:194])
                        em.op("vector", tokas, reads=[ps_tok_b],
                              writes=[tok_sb_b[tsl]])
                        ow_o = owh[:, sl, T, 0:64]
                        ow_nm = owh[:, sl, T, 64:65]
                        ow_l = owh[:, sl, T, 65:66]
                        em.op("vector",
                              lambda e, tsl=tsl, ow_nm=ow_nm: e.tensor_tensor(
                                  mg_nm[:, tsl, :], ow_nm,
                                  tok_sb[:, tsl, 64:65], ALU.min),
                              reads=[owh_b[sl], tok_sb_b[tsl]],
                              writes=[nm_b[tsl]])
                        def dcalc(e, tsl=tsl, ow_nm=ow_nm):
                            e.tensor_tensor(mg_d[:, tsl, 0:1],
                                            mg_nm[:, tsl, :], ow_nm,
                                            ALU.subtract)
                            return e.tensor_tensor(
                                mg_d[:, tsl, 1:2], mg_nm[:, tsl, :],
                                tok_sb[:, tsl, 64:65], ALU.subtract)
                        em.op("vector", dcalc,
                              reads=[nm_b[tsl], owh_b[sl], tok_sb_b[tsl]],
                              writes=[d_b[tsl]])
                        em.op("scalar",
                              lambda e, tsl=tsl: e.activation(
                                  mg_ab[:, tsl, :], mg_d[:, tsl, :], AF.Exp),
                              reads=[d_b[tsl]], writes=[ab_b[tsl]])
                        em.op("vector",
                              lambda e, tsl=tsl, ow_o=ow_o: e.tensor_scalar_mul(
                                  mg_t[:, tsl, :], ow_o, mg_ab[:, tsl, 0:1]),
                              reads=[owh_b[sl], ab_b[tsl]], writes=[t_b[tsl]])
                        em.op("vector",
                              lambda e, tsl=tsl: e.scalar_tensor_tensor(
                                  mg_o[:, tsl, :], tok_sb[:, tsl, 0:64],
                                  mg_ab[:, tsl, 1:2], mg_t[:, tsl, :],
                                  ALU.mult, ALU.add),
                              reads=[tok_sb_b[tsl], ab_b[tsl], t_b[tsl]],
                              writes=[o_b[tsl]])
                        def lprod(e, tsl=tsl, ow_l=ow_l):
                            e.tensor_tensor(mg_d[:, tsl, 0:1], ow_l,
                                            mg_ab[:, tsl, 0:1], ALU.mult)
                            return e.tensor_tensor(mg_d[:, tsl, 1:2],
                                                   tok_sb[:, tsl, 65:66],
                                                   mg_ab[:, tsl, 1:2],
                                                   ALU.mult)
                        em.op("vector", lprod,
                              reads=[owh_b[sl], tok_sb_b[tsl], ab_b[tsl]],
                              writes=[d_b[tsl]])
                        em.op("vector",
                              lambda e, tsl=tsl: e.reduce_sum(
                                  mg_l[:, tsl, :], mg_d[:, tsl, :], axis=AX.X),
                              reads=[d_b[tsl]], writes=[l_b[tsl]])
                        em.op("vector",
                              lambda e, tsl=tsl: e.reciprocal(
                                  rl2[:, tsl, :], mg_l[:, tsl, :]),
                              reads=[l_b[tsl]], writes=[rl2_b[tsl]])
                        em.op("vector",
                              lambda e, tsl=tsl: e.tensor_scalar_mul(
                                  attnb[:, tsl, :], mg_o[:, tsl, :],
                                  rl2[:, tsl, :]),
                              reads=[o_b[tsl], rl2_b[tsl]],
                              writes=[attnb_b[tsl]])
                        em.op("tensor",
                              lambda e, tsl=tsl: e.transpose(
                                  ps_aob[:, 0:128], attnb[:, tsl, :],
                                  ident[:]),
                              reads=[attnb_b[tsl], ident_b],
                              writes=[ps_aob_b])
                        base = (h % 2) * 64
                        em.op("vector",
                              lambda e, base=base, h=h, T=T: e.tensor_copy(
                                  aoT[base:base + 64, h // 2,
                                      T * 128:(T + 1) * 128],
                                  ps_aob[:, 0:128]),
                              reads=[ps_aob_b], writes=[aoT_b])
                if dbg:
                    em.op("scalar", lambda e: e.dma_start(dbg_ao_h, aoT[:]),
                          reads=[aoT_b], dma=True)
            em.run_phase("A2b")
            if upto == "a2b":
                return nc, em

            em.begin_phase()
            with contextlib.ExitStack() as ph:
                ec = ph.enter_context
                wo_sb = ec(nc.sbuf_tensor("wo_sb", [128, 8, D], BF16))
                x_t3 = ec(nc.sbuf_tensor("x_t3", [128, 2, D], F32))
                ln2 = ec(nc.sbuf_tensor("ln2", [128, D], F32))
                x2t = ec(nc.sbuf_tensor("x2t", [128, 2, D], F32))
                ln3 = ec(nc.sbuf_tensor("ln3", [128, D], BF16))
                ln3Ts = ec(nc.sbuf_tensor("ln3Ts", [128, 2, 8, 128], BF16))
                ps_pr = ec(nc.psum_tensor("ps_pr", [128, 2, 512], F32))
                ps_tr3 = ec(nc.psum_tensor("ps_tr3", [128, 2, 1024], BF16))
                wo_b = Buf("wo")
                x_t3_b = [Buf("x30"), Buf("x31")]
                ln2_b = Buf("ln2")
                x2t_b = [Buf("x20"), Buf("x21")]
                ln3_b = Buf("ln3")
                ln3Ts_b = [Buf("l3T0"), Buf("l3T1")]
                ps_pr_b = [Buf("ppr0"), Buf("ppr1")]
                ps_tr3_b = [Buf("ptr30"), Buf("ptr31")]

                em.op("gpsimd",
                      lambda e: e.dma_start(
                          wo_sb[:], wo_h.rearrange("(c p) o -> p c o", p=128)),
                      writes=[wo_b], dma=True)
                for T in range(NT):
                    sl = T % 2
                    c0 = T * 128
                    em.op("sync",
                          lambda e, sl=sl, c0=c0: e.dma_start(
                              x_t3[:, sl, :], x_h[c0:c0 + 128, :]),
                          writes=[x_t3_b[sl]], dma=True)
                    for oh in range(2):
                        psl = oh
                        def prmm(e, psl=psl, oh=oh, c0=c0):
                            for fg in range(8):
                                ins = e.matmul(
                                    ps_pr[:, psl, :], aoT[:, fg, c0:c0 + 128],
                                    wo_sb[:, fg, oh * 512:(oh + 1) * 512],
                                    start=(fg == 0), stop=(fg == 7))
                            return ins
                        em.op("tensor", prmm, reads=[wo_b, aoT_b],
                              writes=[ps_pr_b[psl]])
                        em.op("vector",
                              lambda e, psl=psl, oh=oh: e.tensor_copy(
                                  ln2[:, oh * 512:(oh + 1) * 512],
                                  ps_pr[:, psl, :]),
                              reads=[ps_pr_b[psl]], writes=[ln2_b])
                    rstd, negmr, rstd_b, negmr_b = ln_stats(ln2[:], [ln2_b])
                    em.op("scalar",
                          lambda e, r=rstd, m=negmr: e.activation(
                              ln2[:], ln2[:], AF.Identity, bias=m, scale=r),
                          reads=[ln2_b, rstd_b, negmr_b], writes=[ln2_b])
                    em.op("vector",
                          lambda e, sl=sl: e.tensor_add(
                              x2t[:, sl, :], x_t3[:, sl, :], ln2[:]),
                          reads=[x_t3_b[sl], ln2_b], writes=[x2t_b[sl]])
                    em.op("scalar",
                          lambda e, sl=sl, c0=c0: e.dma_start(
                              x2_h[c0:c0 + 128, :], x2t[:, sl, :]),
                          reads=[x2t_b[sl]], dma=True)
                    rstd, negmr, rstd_b, negmr_b = ln_stats(
                        x2t[:, sl, :], [x2t_b[sl]])
                    em.op("scalar",
                          lambda e, sl=sl, r=rstd, m=negmr: e.activation(
                              ln3[:], x2t[:, sl, :], AF.Identity, bias=m,
                              scale=r),
                          reads=[x2t_b[sl], rstd_b, negmr_b], writes=[ln3_b])
                    for f in range(8):
                        psl = f % 2
                        em.op("tensor",
                              lambda e, f=f, psl=psl: e.transpose(
                                  ps_tr3[:, psl, 0:128],
                                  ln3[:, f * 128:(f + 1) * 128], ident[:]),
                              reads=[ln3_b, ident_b], writes=[ps_tr3_b[psl]])
                        em.op("vector",
                              lambda e, f=f, sl=sl, psl=psl: e.tensor_copy(
                                  ln3Ts[:, sl, f, :], ps_tr3[:, psl, 0:128]),
                              reads=[ps_tr3_b[psl]], writes=[ln3Ts_b[sl]])
                    em.op("scalar",
                          lambda e, sl=sl, c0=c0: e.dma_start(
                              ln3T_hv[:, :, c0:c0 + 128], ln3Ts[:, sl, :, :]),
                          reads=[ln3Ts_b[sl]], dma=True)
            em.run_phase("A3")
            if upto == "a3":
                return nc, em

        em.begin_phase()
        with contextlib.ExitStack() as ph:
            ec = ph.enter_context
            s1w_c = ec(nc.sbuf_tensor("s1w_c", [128, 2, FH, 128], BF16))
            s2w_c = ec(nc.sbuf_tensor("s2w_c", [128, 2, FH, 4, 128], BF16))
            acc = ec(nc.sbuf_tensor("acc", [128, 16, D], F32))
            ln3c = ec(nc.sbuf_tensor("ln3c", [128, 2, 2048], BF16))
            wcp = ec(nc.sbuf_tensor("wcp", [128, 2, 16, D], BF16))
            ybf = ec(nc.sbuf_tensor("ybf", [128, 2, 4, 512], BF16))
            g_s = ec(nc.sbuf_tensor("g_s", [128, 2, 512], BF16))
            hT = ec(nc.sbuf_tensor("hT", [128, 2, 16, 512], BF16))
            x2b = ec(nc.sbuf_tensor("x2b", [128, 2, D], F32))
            lnf = ec(nc.sbuf_tensor("lnf", [128, D], F32))
            ps_o = ec(nc.psum_tensor("ps_o", [128, 2, 1024], F32))
            ps_y = ec(nc.psum_tensor("ps_y", [128, 512], F32))
            ps_z = ec(nc.psum_tensor("ps_z", [128, 2, 512], F32))

            s1w_b = [Buf("s1wa"), Buf("s1wb")]
            s2w_b = [Buf("s2wa"), Buf("s2wb")]
            acc_b = [Buf("acc%d" % i) for i in range(16)]
            ln3c_b = [Buf("l3c0"), Buf("l3c1")]
            wcp_b = [Buf("wcpa"), Buf("wcpb")]
            ybf_b = [Buf("ybf0"), Buf("ybf1")]
            g_s_b = [Buf("gs0"), Buf("gs1")]
            hT_b = [Buf("hT0"), Buf("hT1")]
            x2b_b = [Buf("x2b0"), Buf("x2b1")]
            lnf_b = Buf("lnf")
            ps_o_b = [Buf("pso0"), Buf("pso1")]
            ps_y_b = Buf("psy")
            ps_z_b = [Buf("psz0"), Buf("psz1")]

            wc_hv = wc_h.rearrange("(pc p) o -> pc p o", p=128)
            s1src = s1w_h.rearrange("h c (pp p) m -> c p h (pp m)", p=128)
            s2src = s2w_h.rearrange("h c q (pp p) m -> c h p q (pp m)", p=128)
            def chunk_loads(cidx):
                half_, c_ = cidx // NCHUNK, cidx % NCHUNK
                cset_ = cidx % 2
                t0h_ = half_ * 2048
                em.op("sync",
                      lambda e, c=c_, cset=cset_, t0h=t0h_: e.dma_start(
                          ln3c[:, cset, :], ln3T_hv[:, c, t0h:t0h + 2048]),
                      writes=[ln3c_b[cset_]], dma=True)
                em.op("gpsimd",
                      lambda e, c=c_, cset=cset_: e.dma_start(
                          s1w_c[:, cset, :, :], s1src[c]),
                      writes=[s1w_b[cset_]], dma=True)
                for fh in range(FH):
                    em.op("gpsimd",
                          lambda e, c=c_, cset=cset_, fh=fh: e.dma_start(
                              s2w_c[:, cset, fh, :, :], s2src[c, fh]),
                          writes=[s2w_b[cset_]], dma=True)
                for p in range(16):
                    em.op("gpsimd",
                          lambda e, c=c_, cset=cset_, p=p: e.dma_start(
                              wcp[:, cset, p, :], wc_hv[c * 16 + p]),
                          writes=[wcp_b[cset_]], dma=True)

            chunk_loads(0)
            for half in range(2):
                t0h = half * 2048
                for c in range(NCHUNK):
                    cidx = half * NCHUNK + c
                    cset = cidx % 2
                    for st_ in ("tensor", "vector", "scalar"):
                        em.wait_only(st_, [("qG", em.cnt["qG"]),
                                           ("qY", em.cnt["qY"])])
                    if cidx + 1 < 16:
                        chunk_loads(cidx + 1)

                    for G in range(4):
                        gsl = G % 2
                        g0 = G * 512
                        for fh in range(4):
                            em.op("tensor",
                                  lambda e, cset=cset, g0=g0, fh=fh: e.matmul(
                                      ps_y[:], s1w_c[:, cset, fh, :],
                                      ln3c[:, cset, g0:g0 + 512],
                                      start=True, stop=True),
                                  reads=[s1w_b[cset], ln3c_b[cset]],
                                  writes=[ps_y_b])
                            em.op("vector",
                                  lambda e, gsl=gsl, fh=fh: e.tensor_copy(
                                      ybf[:, gsl, fh, :], ps_y[:]),
                                  reads=[ps_y_b], writes=[ybf_b[gsl]])
                        for fh in range(4):
                            for q in range(4):
                                piece = fh * 4 + q
                                zsl = piece % 2
                                em.op("tensor",
                                      lambda e, cset=cset, gsl=gsl, fh=fh, q=q,
                                      zsl=zsl: e.matmul(
                                          ps_z[:, zsl, :],
                                          s2w_c[:, cset, fh, q, :],
                                          ybf[:, gsl, fh, :],
                                          start=True, stop=True),
                                      reads=[s2w_b[cset], ybf_b[gsl]],
                                      writes=[ps_z_b[zsl]])
                                em.op("scalar",
                                      lambda e, zsl=zsl:
                                      e.activation(g_s[:, zsl, :],
                                                   ps_z[:, zsl, :], AF.Gelu),
                                      reads=[ps_z_b[zsl]],
                                      writes=[g_s_b[zsl]])
                                em.op("vector",
                                      lambda e, gsl=gsl, piece=piece, zsl=zsl:
                                      e.tensor_copy(hT[:, gsl, piece, :],
                                                    g_s[:, zsl, :]),
                                      reads=[g_s_b[zsl]], writes=[hT_b[gsl]])
                        for ti in range(4):
                            t = G * 4 + ti
                            tsl = t % 2
                            def cpmm(e, gsl=gsl, cset=cset, ti=ti, tsl=tsl):
                                for p in range(16):
                                    for oh in range(2):
                                        ins = e.matmul(
                                            ps_o[:, tsl,
                                                 oh * 512:(oh + 1) * 512],
                                            hT[:, gsl, p,
                                               ti * 128:(ti + 1) * 128],
                                            wcp[:, cset, p,
                                                oh * 512:(oh + 1) * 512],
                                            start=(p == 0), stop=(p == 15))
                                return ins
                            em.op("tensor", cpmm,
                                  reads=[hT_b[gsl], wcp_b[cset]],
                                  writes=[ps_o_b[tsl]])
                            if c == 0:
                                em.op("vector",
                                      lambda e, tsl=tsl, t=t: e.tensor_copy(
                                          acc[:, t, :], ps_o[:, tsl, :]),
                                      reads=[ps_o_b[tsl]], writes=[acc_b[t]])
                            else:
                                em.op("vector",
                                      lambda e, tsl=tsl, t=t: e.tensor_tensor(
                                          acc[:, t, :], acc[:, t, :],
                                          ps_o[:, tsl, :], ALU.add),
                                      reads=[ps_o_b[tsl], acc_b[t]],
                                      writes=[acc_b[t]])
                if dbg:
                    em.op("scalar",
                          lambda e, half=half: e.dma_start(
                              dbg_acc_h[half], acc[:]),
                          reads=list(acc_b), dma=True)
                for t in range(16):
                    tt = t0h + t * 128
                    sl = t % 2
                    em.op("sync",
                          lambda e, sl=sl, tt=tt: e.dma_start(
                              x2b[:, sl, :], x2_h[tt:tt + 128, :]),
                          writes=[x2b_b[sl]], dma=True)
                    rstd, negmr, rstd_b, negmr_b = ln_stats(
                        acc[:, t, :], [acc_b[t]])
                    em.op("scalar",
                          lambda e, t=t, r=rstd, m=negmr: e.activation(
                              lnf[:], acc[:, t, :], AF.Identity, bias=m,
                              scale=r),
                          reads=[acc_b[t], rstd_b, negmr_b], writes=[lnf_b])
                    em.op("vector",
                          lambda e, sl=sl: e.tensor_add(
                              x2b[:, sl, :], x2b[:, sl, :], lnf[:]),
                          reads=[x2b_b[sl], lnf_b], writes=[x2b_b[sl]])
                    em.op("scalar",
                          lambda e, sl=sl, tt=tt: e.dma_start(
                              y_h[tt:tt + 128, :], x2b[:, sl, :]),
                          reads=[x2b_b[sl]], dma=True)
        em.run_phase("B")
    return nc, em



def _build_masks():
    r = np.arange(128)[:, None]
    c = np.arange(192)[None, :]
    m_win = np.where((c >= r) & (c <= r + 64), 0.0, NEG).astype(np.float32)
    c0 = np.arange(128)[None, :]
    m_win0 = np.where((c0 <= r) & (c0 >= r - 64), 0.0, NEG).astype(np.float32)
    m_fix = np.full((NT, 128, 192), NEG, dtype=np.float32)
    diag = (c0 // 64 == r // 64) & (c0 <= r)
    m = np.arange(64)[None, :]
    for T in range(NT):
        m_fix[T, :, 0:128] = np.where(diag, 0.0, NEG)
        m_fix[T, :, 128:192] = np.where(m <= (2 * T + r // 64 - 1), 0.0, NEG)
    b = np.arange(64)[:, None]
    m_ph = np.where(m <= b - 2, 0.0, NEG).astype(np.float32)
    return m_win, m_win0, m_fix, m_ph


def _prep_inputs(L0_attn_x, L0_attn_Wq, L0_attn_Wk, L0_attn_Wv, L0_attn_Wo,
                 L0_ffn_rail_maker, L0_ffn_rail_up_proj, L0_ffn_w_cproj):
    x8 = np.ascontiguousarray(np.asarray(L0_attn_x, dtype=np.float32))
    wqT = np.ascontiguousarray(np.asarray(L0_attn_Wq, dtype=np.float32).T)
    wkT = np.ascontiguousarray(np.asarray(L0_attn_Wk, dtype=np.float32).T)
    wvT = np.ascontiguousarray(np.asarray(L0_attn_Wv, dtype=np.float32).T)
    woT = np.ascontiguousarray(np.asarray(L0_attn_Wo, dtype=np.float32).T)
    rm = np.asarray(L0_ffn_rail_maker, dtype=np.float32)
    rup = np.asarray(L0_ffn_rail_up_proj, dtype=np.float32)
    wc = np.asarray(L0_ffn_w_cproj, dtype=np.float32)

    s1w = np.zeros((FH, NCHUNK, 128, 128), np.float32)
    rmg = rm.reshape(FH, NCHUNK, 8, 16, 16)
    for rp in range(8):
        s1w[:, :, rp * 16:(rp + 1) * 16, rp * 16:(rp + 1) * 16] = rmg[:, :, rp]
    s2w = np.zeros((FH, NCHUNK, 4, 128, 128), np.float32)
    rupg = rup.reshape(FH, NCHUNK, 8, 4, 16, 16)
    for rp in range(8):
        s2w[:, :, :, rp * 16:(rp + 1) * 16, rp * 16:(rp + 1) * 16] = \
            rupg[:, :, rp].transpose(0, 1, 2, 4, 3)

    wct = np.ascontiguousarray(wc.T)
    R = np.arange(FF)
    cch = R // 2048
    rem = R % 2048
    piece = rem // 128
    mm_ = rem % 128
    hh = piece // 4
    qq = piece % 4
    rr = 8 * cch + mm_ // 16
    kk = 16 * qq + mm_ % 16
    wcT = np.ascontiguousarray(wct[hh * 4096 + rr * 64 + kk])

    m_win, m_win0, m_fix, m_ph = _build_masks()
    ident = np.eye(128, dtype=ml_dtypes.bfloat16)
    shared = dict(wqT=wqT, wkT=wkT, wvT=wvT, woT=woT, s1w=s1w, s2w=s2w,
                  wcT=wcT, m_win=m_win, m_win0=m_win0, m_fix=m_fix,
                  m_ph=m_ph, ident_c=ident)
    return x8, shared


def kernel(L0_attn_x, L0_attn_Wq, L0_attn_Wk, L0_attn_Wv, L0_attn_Wo,
           L0_ffn_rail_maker, L0_ffn_rail_up_proj, L0_ffn_w_cproj):
    x8, shared = _prep_inputs(L0_attn_x, L0_attn_Wq, L0_attn_Wk, L0_attn_Wv,
                              L0_attn_Wo, L0_ffn_rail_maker,
                              L0_ffn_rail_up_proj, L0_ffn_w_cproj)
    nc, _ = build_program("all")
    in_maps = [dict(shared, x=np.ascontiguousarray(x8[i])) for i in range(8)]
    res = run_bass_kernel_spmd(nc, in_maps, core_ids=list(range(8)))
    out = np.stack([res.results[i]["y"] for i in range(8)], axis=0)
    return out.astype(np.float32)


# version 06
# speedup vs baseline: 1.2583x; 1.2583x over previous
import contextlib
import numpy as np
import ml_dtypes
import concourse.bass as bass
import concourse.mybir as mybir
from concourse.bass_utils import run_bass_kernel_spmd

F32 = mybir.dt.float32
BF16 = mybir.dt.bfloat16
AF = mybir.ActivationFunctionType
ALU = mybir.AluOpType
AX = mybir.AxisListType

S = 4096
D = 1024
NT = 32
NH = 16
NEG = -1.0e30
EPS = 1e-5
QSCALE = 1.0 / 8.0

FH = 4
NCHUNK = 8
FF = 16384



STREAMS = ("tensor", "vector", "scalar", "gpsimd", "sync")
SEM_OF = {"tensor": "T", "vector": "V", "scalar": "S", "gpsimd": "G"}
DMA_SEM_OF = {"sync": "qY", "gpsimd": "qG", "scalar": "qS"}


class Buf:
    __slots__ = ("name", "w", "r")

    def __init__(self, name):
        self.name = name
        self.w = []
        self.r = []


class Em:
    def __init__(self, nc):
        self.nc = nc
        self.sem_names = ["T", "V", "S", "G", "qY", "qG", "qS"]
        self.sems = {}
        self.cnt = {n: 0 for n in self.sem_names}
        self.last_wait = {s: {} for s in STREAMS}
        self.thunks = None
        self.nops = 0

    def begin_phase(self):
        self.thunks = {s: [] for s in STREAMS}
        for s in STREAMS:
            self.last_wait[s] = dict(self.cnt)

    def _waits_for(self, stream, deps):
        need = {}
        for (sem, val) in deps:
            if stream == "tensor" and sem == "T":
                continue
            if need.get(sem, 0) < val:
                need[sem] = val
        lw = self.last_wait[stream]
        out = []
        for sem, val in need.items():
            if lw.get(sem, 0) >= val:
                continue
            lw[sem] = val
            out.append((sem, val))
        return out

    def op(self, stream, fn, reads=(), writes=(), dma=False):
        deps = []
        for b in reads:
            deps += b.w
        for b in writes:
            deps += b.w
            deps += b.r
        waits = self._waits_for(stream, deps)
        if dma:
            sem, inc = DMA_SEM_OF[stream], 16
        else:
            sem, inc = SEM_OF[stream], 1
        self.cnt[sem] += inc
        token = (sem, self.cnt[sem])
        sems = self.sems

        def thunk(e, waits=waits, fn=fn, sem=sem, inc=inc):
            for (s_, v_) in waits:
                e.wait_ge(sems[s_], v_)
            fn(e).then_inc(sems[sem], inc)

        self.thunks[stream].append(thunk)
        self.nops += 1
        for b in writes:
            b.w = [token]
            b.r = []
        for b in reads:
            b.r.append(token)
        return token

    def wait_only(self, stream, deps):
        waits = self._waits_for(stream, deps)
        if not waits:
            return
        sems = self.sems

        def thunk(e, waits=waits):
            for (s_, v_) in waits:
                e.wait_ge(sems[s_], v_)

        self.thunks[stream].append(thunk)

    def run_phase(self, name):
        for stream, sem in DMA_SEM_OF.items():
            self.wait_only(stream, [(sem, self.cnt[sem])])
        th = self.thunks
        with self.nc.named_scope(name), self.nc.Block(name) as block:
            if th["sync"]:
                @block.sync
                def _(e, th=th):
                    for t in th["sync"]:
                        t(e)
            if th["gpsimd"]:
                @block.gpsimd
                def _(e, th=th):
                    for t in th["gpsimd"]:
                        t(e)
            if th["tensor"]:
                @block.tensor
                def _(e, th=th):
                    for t in th["tensor"]:
                        t(e)
            if th["vector"]:
                @block.vector
                def _(e, th=th):
                    for t in th["vector"]:
                        t(e)
            if th["scalar"]:
                @block.scalar
                def _(e, th=th):
                    for t in th["scalar"]:
                        t(e)
        self.thunks = None



def build_program(upto="all", dbg=False):
    nc = bass.Bass("TRN2", target_bir_lowering=False)
    em = Em(nc)
    KIND = "ExternalOutput" if dbg else "Internal"

    x_h = nc.dram_tensor("x", [S, D], F32, kind="ExternalInput").ap()
    wq_h = nc.dram_tensor("wqT", [D, D], F32, kind="ExternalInput").ap()
    wk_h = nc.dram_tensor("wkT", [D, D], F32, kind="ExternalInput").ap()
    wv_h = nc.dram_tensor("wvT", [D, D], F32, kind="ExternalInput").ap()
    wo_h = nc.dram_tensor("woT", [D, D], F32, kind="ExternalInput").ap()
    s1w_h = nc.dram_tensor("s1w", [FH, NCHUNK, 128, 128], F32,
                           kind="ExternalInput").ap()
    s2w_h = nc.dram_tensor("s2w", [FH, NCHUNK, 4, 128, 128], F32,
                           kind="ExternalInput").ap()
    wc_h = nc.dram_tensor("wcT", [FF, D], F32, kind="ExternalInput").ap()
    mwin_h = nc.dram_tensor("m_win", [128, 192], F32, kind="ExternalInput").ap()
    mwin0_h = nc.dram_tensor("m_win0", [128, 128], F32, kind="ExternalInput").ap()
    mfix_h = nc.dram_tensor("m_fix", [NT, 128, 192], F32, kind="ExternalInput").ap()
    mph_h = nc.dram_tensor("m_ph", [64, 64], F32, kind="ExternalInput").ap()
    mph2_h = nc.dram_tensor("m_ph2", [64, 128], F32, kind="ExternalInput").ap()
    ident_h = nc.dram_tensor("ident_c", [128, 128], BF16, kind="ExternalInput").ap()
    y_h = nc.dram_tensor("y", [S, D], F32, kind="ExternalOutput").ap()

    qT_h = nc.dram_tensor("qT_f", [D, S], BF16, kind=KIND).ap()
    kT_h = nc.dram_tensor("kT_f", [D, S], BF16, kind=KIND).ap()
    vT_h = nc.dram_tensor("vT_f", [D, S], BF16, kind=KIND).ap()
    vtok_h = nc.dram_tensor("vtok", [NT, 128, D], BF16, kind=KIND).ap()
    ow_h = nc.dram_tensor("ow", [8, NT, 128, 66], F32, kind=KIND).ap()
    ln3T_h = nc.dram_tensor("ln3T", [D, S], BF16, kind=KIND).ap()
    x2_h = nc.dram_tensor("x2", [S, D], F32, kind=KIND).ap()
    if dbg:
        dbg_ao_h = nc.dram_tensor("dbg_ao", [128, 8, S], BF16,
                                  kind="ExternalOutput").ap()
        dbg_opt_h = nc.dram_tensor("dbg_opt", [66, S], F32,
                                   kind="ExternalOutput").ap()
        dbg_tok_h = nc.dram_tensor("dbg_tok", [128, 66], F32,
                                   kind="ExternalOutput").ap()
        dbg_ml_h = nc.dram_tensor("dbg_ml", [128, 8], F32,
                                  kind="ExternalOutput").ap()
        dbg_acc_h = nc.dram_tensor("dbg_acc", [2, 128, 16, D], F32,
                                   kind="ExternalOutput").ap()

    qT_hv = qT_h.rearrange("(c p) t -> p c t", p=128)
    kT_hv = kT_h.rearrange("(c p) t -> p c t", p=128)
    ln3T_hv = ln3T_h.rearrange("(c p) t -> p c t", p=128)

    stack = contextlib.ExitStack()
    with stack:
        for n in em.sem_names:
            em.sems[n] = stack.enter_context(nc.semaphore("sem_" + n))

        ident = stack.enter_context(nc.sbuf_tensor("ident", [128, 128], BF16))
        ident32 = stack.enter_context(nc.sbuf_tensor("ident32", [128, 128], F32))
        ident_b, ident32_b = Buf("ident"), Buf("ident32")
        st_t = {}
        st_b = {}
        for n in ("sums", "sumsq", "t_m", "t_var", "t_sd", "t_rstd", "t_negmr"):
            st_t[n] = stack.enter_context(nc.sbuf_tensor("st_" + n, [128, 1], F32))
            st_b[n] = Buf(n)
        scr = stack.enter_context(nc.sbuf_tensor("scr", [128, D], BF16))
        scr_b = Buf("scr")

        def ln_stats(src_ap, src_bufs, W=D):
            inv = 1.0 / W
            em.op("scalar",
                  lambda e: e.activation(scr[:, 0:W], src_ap, AF.Square,
                                         accum_out=st_t["sumsq"][:]),
                  reads=src_bufs, writes=[st_b["sumsq"], scr_b])
            em.op("vector",
                  lambda e: e.reduce_sum(st_t["sums"][:], src_ap, axis=AX.X),
                  reads=src_bufs, writes=[st_b["sums"]])
            em.op("vector",
                  lambda e: e.tensor_scalar_mul(st_t["t_m"][:], st_t["sums"][:],
                                                inv),
                  reads=[st_b["sums"]], writes=[st_b["t_m"]])
            em.op("vector",
                  lambda e: e.tensor_scalar(
                      st_t["t_var"][:], st_t["t_m"][:], st_t["t_m"][:], -EPS,
                      op0=ALU.mult, op1=ALU.add),
                  reads=[st_b["t_m"]], writes=[st_b["t_var"]])
            em.op("vector",
                  lambda e: e.scalar_tensor_tensor(
                      st_t["t_var"][:], st_t["sumsq"][:], inv, st_t["t_var"][:],
                      ALU.mult, ALU.subtract),
                  reads=[st_b["sumsq"], st_b["t_var"]], writes=[st_b["t_var"]])
            em.op("scalar",
                  lambda e: e.activation(st_t["t_sd"][:], st_t["t_var"][:],
                                         AF.Sqrt),
                  reads=[st_b["t_var"]], writes=[st_b["t_sd"]])
            em.op("vector",
                  lambda e: e.reciprocal(st_t["t_rstd"][:], st_t["t_sd"][:]),
                  reads=[st_b["t_sd"]], writes=[st_b["t_rstd"]])
            em.op("vector",
                  lambda e: e.scalar_tensor_tensor(
                      st_t["t_negmr"][:], st_t["t_m"][:], -1.0,
                      st_t["t_rstd"][:], ALU.mult, ALU.mult),
                  reads=[st_b["t_m"], st_b["t_rstd"]], writes=[st_b["t_negmr"]])
            return (st_t["t_rstd"][:], st_t["t_negmr"][:],
                    st_b["t_rstd"], st_b["t_negmr"])

        mid = contextlib.ExitStack()
        with mid:
            aoT = mid.enter_context(nc.sbuf_tensor("aoT", [128, 8, S], BF16))
            aoT_b = Buf("aoT")

            em.begin_phase()
            with contextlib.ExitStack() as ph:
                ec = ph.enter_context
                wq_sb = ec(nc.sbuf_tensor("wq_sb", [128, 8, D], BF16))
                wk_sb = ec(nc.sbuf_tensor("wk_sb", [128, 8, D], BF16))
                wv_sb = ec(nc.sbuf_tensor("wv_sb", [128, 8, D], BF16))
                x_t = ec(nc.sbuf_tensor("x_t", [128, 2, D], F32))
                ln1 = ec(nc.sbuf_tensor("ln1", [128, D], BF16))
                ln1T = ec(nc.sbuf_tensor("ln1T", [128, 2, 8, 512], BF16))
                pstage = ec(nc.sbuf_tensor("pstage", [128, 2, 8, 512], BF16))
                vtstage = ec(nc.sbuf_tensor("vtstage", [128, 4, D], BF16))
                ps_tr1 = ec(nc.psum_tensor("ps_tr1", [128, 2, 1024], BF16))
                ps_mm1 = ec(nc.psum_tensor("ps_mm1", [128, 2, 512], F32))
                wq_b, wk_b, wv_b = Buf("wq"), Buf("wk"), Buf("wv")
                x_t_b = [Buf("x_t0"), Buf("x_t1")]
                ln1_b = Buf("ln1")
                ln1T_b = [Buf("l1T0"), Buf("l1T1")]
                pstage_b = [Buf("ps0"), Buf("ps1")]
                vtstage_b = [Buf("vt%d" % i) for i in range(4)]
                ps_tr1_b = [Buf("ptr0"), Buf("ptr1")]
                ps_mm1_b = [Buf("pmm0"), Buf("pmm1")]

                em.op("sync", lambda e: e.dma_start(ident[:], ident_h),
                      writes=[ident_b], dma=True)
                em.op("vector", lambda e: e.tensor_copy(ident32[:], ident[:]),
                      reads=[ident_b], writes=[ident32_b])
                for wsb, wb, wh in ((wq_sb, wq_b, wq_h), (wk_sb, wk_b, wk_h),
                                    (wv_sb, wv_b, wv_h)):
                    src = wh.rearrange("(c p) o -> p c o", p=128)
                    em.op("gpsimd", lambda e, d=wsb, s=src: e.dma_start(d[:], s),
                          writes=[wb], dma=True)

                for G in range(8):
                    gsl = G % 2
                    for ti in range(4):
                        T = G * 4 + ti
                        sl = T % 2
                        xa = x_t[:, sl, :]
                        em.op("sync",
                              lambda e, xa=xa, T=T: e.dma_start(
                                  xa, x_h[T * 128:(T + 1) * 128, :]),
                              writes=[x_t_b[sl]], dma=True)
                        rstd, negmr, rstd_b, negmr_b = ln_stats(xa, [x_t_b[sl]])
                        em.op("scalar",
                              lambda e, xa=xa, r=rstd, m=negmr: e.activation(
                                  ln1[:], xa, AF.Identity, bias=m, scale=r),
                              reads=[x_t_b[sl], rstd_b, negmr_b],
                              writes=[ln1_b])
                        for f in range(8):
                            psl = f % 2
                            em.op("tensor",
                                  lambda e, f=f, psl=psl: e.transpose(
                                      ps_tr1[:, psl, 0:128],
                                      ln1[:, f * 128:(f + 1) * 128], ident[:]),
                                  reads=[ln1_b, ident_b],
                                  writes=[ps_tr1_b[psl]])
                            em.op("vector",
                                  lambda e, f=f, ti=ti, psl=psl, gsl=gsl:
                                  e.tensor_copy(
                                      ln1T[:, gsl, f, ti * 128:(ti + 1) * 128],
                                      ps_tr1[:, psl, 0:128]),
                                  reads=[ps_tr1_b[psl]], writes=[ln1T_b[gsl]])
                    for pi, (wsb, wb, outdst) in enumerate(
                            ((wq_sb, wq_b, qT_hv), (wk_sb, wk_b, kT_hv),
                             (wv_sb, wv_b, None))):
                        sl = pi % 2
                        for og in range(8):
                            psl = og % 2
                            def mmgrp(e, wsb=wsb, og=og, psl=psl, gsl=gsl):
                                for fg in range(8):
                                    ins = e.matmul(
                                        ps_mm1[:, psl, :],
                                        wsb[:, fg, og * 128:(og + 1) * 128],
                                        ln1T[:, gsl, fg, :],
                                        start=(fg == 0), stop=(fg == 7))
                                return ins
                            em.op("tensor", mmgrp, reads=[wb, ln1T_b[gsl]],
                                  writes=[ps_mm1_b[psl]])
                            if pi == 0:
                                em.op("vector",
                                      lambda e, og=og, psl=psl, sl=sl:
                                      e.tensor_scalar_mul(
                                          pstage[:, sl, og, :],
                                          ps_mm1[:, psl, :], QSCALE),
                                      reads=[ps_mm1_b[psl]],
                                      writes=[pstage_b[sl]])
                            else:
                                em.op("vector",
                                      lambda e, og=og, psl=psl, sl=sl:
                                      e.tensor_copy(pstage[:, sl, og, :],
                                                    ps_mm1[:, psl, :]),
                                      reads=[ps_mm1_b[psl]],
                                      writes=[pstage_b[sl]])
                            if pi == 2:
                                for ti in range(4):
                                    T = G * 4 + ti
                                    tsl = (og * 4 + ti) % 2
                                    em.op("tensor",
                                          lambda e, og=og, ti=ti, sl=sl, tsl=tsl:
                                          e.transpose(
                                              ps_tr1[:, tsl, 0:128],
                                              pstage[:, sl, og,
                                                     ti * 128:(ti + 1) * 128],
                                              ident[:]),
                                          reads=[pstage_b[sl], ident_b],
                                          writes=[ps_tr1_b[tsl]])
                                    em.op("vector",
                                          lambda e, og=og, ti=ti, tsl=tsl:
                                          e.tensor_copy(
                                              vtstage[:, ti,
                                                      og * 128:(og + 1) * 128],
                                              ps_tr1[:, tsl, 0:128]),
                                          reads=[ps_tr1_b[tsl]],
                                          writes=[vtstage_b[ti]])
                        if pi < 2:
                            dst = outdst[:, :, G * 512:(G + 1) * 512]
                            em.op("scalar",
                                  lambda e, dst=dst, sl=sl: e.dma_start(
                                      dst, pstage[:, sl, :, :]),
                                  reads=[pstage_b[sl]], dma=True)
                        else:
                            dst = vT_h.rearrange("(c p) t -> p c t", p=128)[
                                :, :, G * 512:(G + 1) * 512]
                            em.op("scalar",
                                  lambda e, dst=dst, sl=sl: e.dma_start(
                                      dst, pstage[:, sl, :, :]),
                                  reads=[pstage_b[sl]], dma=True)
                            for ti in range(4):
                                T = G * 4 + ti
                                em.op("scalar",
                                      lambda e, T=T, ti=ti: e.dma_start(
                                          vtok_h[T], vtstage[:, ti, :]),
                                      reads=[vtstage_b[ti]], dma=True)
            em.run_phase("A1")
            if upto == "a1":
                return nc, em

            em.begin_phase()
            with contextlib.ExitStack() as ph:
                ec = ph.enter_context
                kwin = ec(nc.sbuf_tensor("kwin", [128, 2, 8, 192], BF16))
                qt = ec(nc.sbuf_tensor("qt", [128, 2, 8, 128], BF16))
                vprev = ec(nc.sbuf_tensor("vprev", [64, 2, D], BF16))
                vcur = ec(nc.sbuf_tensor("vcur", [128, 2, D], BF16))
                ksum = ec(nc.sbuf_tensor("ksum", [128, 4, 64], BF16))
                vsum = ec(nc.sbuf_tensor("vsum", [64, 8, 64], BF16))
                ptmp = ec(nc.sbuf_tensor("ptmp", [128, S], BF16))
                s_sb = ec(nc.sbuf_tensor("s_sb", [128, 2, 192], F32))
                p_sb = ec(nc.sbuf_tensor("p_sb", [128, 2, 192], BF16))
                ptsb = ec(nc.sbuf_tensor("ptsb", [128, 2, 256], BF16))
                owst = ec(nc.sbuf_tensor("owst", [128, 2, 8, 66], F32))
                attn = ec(nc.sbuf_tensor("attn", [128, 2, 64], BF16))
                rl = ec(nc.sbuf_tensor("rl", [128, 2, 1], F32))
                fxs = ec(nc.sbuf_tensor("fxs", [128, 2, 2], F32))
                mwin = ec(nc.sbuf_tensor("mwin", [128, 192], F32))
                mwin0 = ec(nc.sbuf_tensor("mwin0", [128, 128], F32))
                mfix = ec(nc.sbuf_tensor("mfix", [128, NT, 192], F32))
                ps_sc = ec(nc.psum_tensor("ps_sc", [128, 2, 512], F32))
                ps_pt = ec(nc.psum_tensor("ps_pt", [128, 2, 1024], BF16))
                ps_av = ec(nc.psum_tensor("ps_av", [128, 2, 512], F32))
                ps_ao = ec(nc.psum_tensor("ps_ao", [64, 2, 1024], BF16))
                kwin_b = [Buf("kw0"), Buf("kw1")]
                qt_b = [Buf("qt0"), Buf("qt1")]
                vprev_b = [Buf("vp0"), Buf("vp1")]
                vcur_b = [Buf("vc0"), Buf("vc1")]
                ksum_b, vsum_b, ptmp_b = Buf("ksum"), Buf("vsum"), Buf("ptmp")
                s_b = [Buf("s0"), Buf("s1")]
                p_b = [Buf("p0"), Buf("p1")]
                ptsb_b = [Buf("pt0"), Buf("pt1")]
                owst_b = [Buf("ow0"), Buf("ow1")]
                attn_b = [Buf("at0"), Buf("at1")]
                rl_b = [Buf("rl0"), Buf("rl1")]
                fxs_b = [Buf("fx0"), Buf("fx1")]
                mwin_b, mwin0_b, mfix_b = Buf("mwin"), Buf("mwin0"), Buf("mfix")
                ps_sc_b = [Buf("psc0"), Buf("psc1")]
                ps_pt_b = [Buf("ppt0"), Buf("ppt1")]
                ps_av_b = [Buf("pav0"), Buf("pav1")]
                ps_ao_b = [Buf("pao0"), Buf("pao1")]

                em.op("sync", lambda e: e.dma_start(mwin[:], mwin_h),
                      writes=[mwin_b], dma=True)
                em.op("sync", lambda e: e.dma_start(mwin0[:], mwin0_h),
                      writes=[mwin0_b], dma=True)
                em.op("sync", lambda e: e.dma_start(
                    mfix[:], mfix_h.rearrange("t p c -> p t c")),
                      writes=[mfix_b], dma=True)
                for h in range(8, 16):
                    base = (h % 2) * 64
                    em.op("sync",
                          lambda e, h=h, base=base: e.dma_start(
                              ptmp[base:base + 64, :],
                              kT_h[h * 64:(h + 1) * 64, :]),
                          writes=[ptmp_b], dma=True)
                    kv = ptmp.ap().rearrange("p (m ph) -> p m ph", ph=64)
                    em.op("vector",
                          lambda e, h=h, base=base, kv=kv: e.tensor_copy(
                              ksum[base:base + 64, (h - 8) // 2, :],
                              kv[base:base + 64, :, 63]),
                          reads=[ptmp_b], writes=[ksum_b])
                for h in range(8, 16):
                    em.op("sync",
                          lambda e, h=h: e.dma_start(
                              ptmp[0:64, :], vT_h[h * 64:(h + 1) * 64, :]),
                          writes=[ptmp_b], dma=True)
                    vv = ptmp.ap().rearrange("p (m ph) -> p m ph", ph=64)
                    em.op("tensor",
                          lambda e, vv=vv: e.transpose(
                              ps_ao[0:64, 0, 0:64], vv[0:64, :, 63],
                              ident[0:64, 0:64]),
                          reads=[ptmp_b, ident_b], writes=[ps_ao_b[0]])
                    em.op("vector",
                          lambda e, h=h: e.tensor_copy(
                              vsum[:, h - 8, :], ps_ao[0:64, 0, 0:64]),
                          reads=[ps_ao_b[0]], writes=[vsum_b])

                for T in range(NT):
                    sl = T % 2
                    c0 = T * 128
                    em.op("sync",
                          lambda e, sl=sl, c0=c0: e.dma_start(
                              qt[:, sl, :, :], qT_hv[:, :, c0:c0 + 128]),
                          writes=[qt_b[sl]], dma=True)
                    if T == 0:
                        em.op("sync",
                              lambda e, sl=sl: e.dma_start(
                                  kwin[:, sl, :, 64:192], kT_hv[:, :, 0:128]),
                              writes=[kwin_b[sl]], dma=True)
                        em.op("sync",
                              lambda e, sl=sl: e.dma_start(
                                  vcur[:, sl, :], vtok_h[0]),
                              writes=[vcur_b[sl]], dma=True)
                    else:
                        em.op("sync",
                              lambda e, sl=sl, c0=c0: e.dma_start(
                                  kwin[:, sl, :, :],
                                  kT_hv[:, :, c0 - 64:c0 + 128]),
                              writes=[kwin_b[sl]], dma=True)
                        em.op("sync",
                              lambda e, sl=sl, T=T: e.dma_start(
                                  vprev[:, sl, :], vtok_h[T - 1, 64:128, :]),
                              writes=[vprev_b[sl]], dma=True)
                        em.op("sync",
                              lambda e, sl=sl, T=T: e.dma_start(
                                  vcur[:, sl, :], vtok_h[T]),
                              writes=[vcur_b[sl]], dma=True)

                    for h in range(NH):
                        asl = (T * NH + h) % 2
                        base = (h % 2) * 64
                        ch = h // 2
                        strided = h < 8
                        qsl = qt[base:base + 64, sl, ch, :]
                        if strided:
                            if T == 0:
                                em.op("tensor",
                                      lambda e, asl=asl, sl=sl, qsl=qsl,
                                      base=base, ch=ch:
                                      e.matmul(ps_sc[:, asl, 64:192], qsl,
                                               kwin[base:base + 64, sl, ch,
                                                    64:192],
                                               start=True, stop=True),
                                      reads=[qt_b[sl], kwin_b[sl]],
                                      writes=[ps_sc_b[asl]])
                                em.op("vector",
                                      lambda e, asl=asl: e.tensor_add(
                                          s_sb[:, asl, 64:192],
                                          ps_sc[:, asl, 64:192], mwin0[:]),
                                      reads=[ps_sc_b[asl], mwin0_b],
                                      writes=[s_b[asl]])
                                r0, r1 = 64, 192
                            else:
                                em.op("tensor",
                                      lambda e, asl=asl, sl=sl, qsl=qsl,
                                      base=base, ch=ch:
                                      e.matmul(ps_sc[:, asl, 0:192], qsl,
                                               kwin[base:base + 64, sl, ch, :],
                                               start=True, stop=True),
                                      reads=[qt_b[sl], kwin_b[sl]],
                                      writes=[ps_sc_b[asl]])
                                em.op("vector",
                                      lambda e, asl=asl: e.tensor_add(
                                          s_sb[:, asl, 0:192],
                                          ps_sc[:, asl, 0:192], mwin[:]),
                                      reads=[ps_sc_b[asl], mwin_b],
                                      writes=[s_b[asl]])
                                r0, r1 = 0, 192
                            nm_ap = owst[:, sl, h, 64:65]
                            l_ap = owst[:, sl, h, 65:66]
                            nm_wb = owst_b[sl]
                            lbuf = owst_b[sl]
                        else:
                            def scmm(e, asl=asl, sl=sl, qsl=qsl, base=base,
                                     ch=ch, h=h):
                                e.matmul(ps_sc[:, asl, 0:128], qsl,
                                         kwin[base:base + 64, sl, ch, 64:192],
                                         start=True, stop=True)
                                return e.matmul(
                                    ps_sc[:, asl, 128:192], qsl,
                                    ksum[base:base + 64, (h - 8) // 2, :],
                                    start=True, stop=True)
                            em.op("tensor", scmm,
                                  reads=[qt_b[sl], kwin_b[sl], ksum_b],
                                  writes=[ps_sc_b[asl]])
                            em.op("vector",
                                  lambda e, asl=asl, T=T: e.tensor_add(
                                      s_sb[:, asl, 0:192],
                                      ps_sc[:, asl, 0:192], mfix[:, T, :]),
                                  reads=[ps_sc_b[asl], mfix_b],
                                  writes=[s_b[asl]])
                            r0, r1 = 0, 192
                            nm_ap = fxs[:, asl, 0:1]
                            l_ap = fxs[:, asl, 1:2]
                            nm_wb = fxs_b[asl]
                            lbuf = fxs_b[asl]
                        em.op("vector",
                              lambda e, asl=asl, r0=r0, r1=r1, nm=nm_ap:
                              e.reduce_max(nm, s_sb[:, asl, r0:r1], axis=AX.X,
                                           negate=True),
                              reads=[s_b[asl]], writes=[nm_wb])
                        em.op("scalar",
                              lambda e, asl=asl, r0=r0, r1=r1, nm=nm_ap,
                              l=l_ap:
                              e.activation(p_sb[:, asl, r0:r1],
                                           s_sb[:, asl, r0:r1], AF.Exp,
                                           bias=nm, accum_out=l),
                              reads=[s_b[asl], nm_wb], writes=[p_b[asl], lbuf])
                        if strided:
                            def ptr(e, asl=asl, T=T):
                                if T > 0:
                                    e.transpose(ps_pt[0:64, asl, 0:128],
                                                p_sb[:, asl, 0:64], ident[:])
                                return e.transpose(ps_pt[:, asl, 128:256],
                                                   p_sb[:, asl, 64:192],
                                                   ident[:])
                        else:
                            def ptr(e, asl=asl):
                                e.transpose(ps_pt[:, asl, 0:128],
                                            p_sb[:, asl, 0:128], ident[:])
                                return e.transpose(ps_pt[0:64, asl, 128:256],
                                                   p_sb[:, asl, 128:192],
                                                   ident[:])
                        em.op("tensor", ptr, reads=[p_b[asl], ident_b],
                              writes=[ps_pt_b[asl]])
                        em.op("vector",
                              lambda e, asl=asl: e.tensor_copy(
                                  ptsb[:, asl, :], ps_pt[:, asl, 0:256]),
                              reads=[ps_pt_b[asl]], writes=[ptsb_b[asl]])
                        if strided:
                            def avmm(e, asl=asl, sl=sl, h=h, T=T):
                                ins = e.matmul(
                                    ps_av[:, asl, 0:64],
                                    ptsb[0:128, asl, 128:256],
                                    vcur[:, sl, h * 64:(h + 1) * 64],
                                    start=True, stop=(T == 0))
                                if T > 0:
                                    ins = e.matmul(
                                        ps_av[:, asl, 0:64],
                                        ptsb[0:64, asl, 0:128],
                                        vprev[:, sl, h * 64:(h + 1) * 64],
                                        start=False, stop=True)
                                return ins
                            em.op("tensor", avmm,
                                  reads=[ptsb_b[asl], vcur_b[sl]] +
                                        ([vprev_b[sl]] if T > 0 else []),
                                  writes=[ps_av_b[asl]])
                            em.op("vector",
                                  lambda e, asl=asl, sl=sl, h=h: e.tensor_copy(
                                      owst[:, sl, h, 0:64],
                                      ps_av[:, asl, 0:64]),
                                  reads=[ps_av_b[asl]], writes=[owst_b[sl]])
                        else:
                            def avmm(e, asl=asl, sl=sl, h=h):
                                e.matmul(ps_av[:, asl, 0:64],
                                         ptsb[0:128, asl, 0:128],
                                         vcur[:, sl, h * 64:(h + 1) * 64],
                                         start=True, stop=False)
                                return e.matmul(ps_av[:, asl, 0:64],
                                                ptsb[0:64, asl, 128:256],
                                                vsum[:, h - 8, :],
                                                start=False, stop=True)
                            em.op("tensor", avmm,
                                  reads=[ptsb_b[asl], vcur_b[sl], vsum_b],
                                  writes=[ps_av_b[asl]])
                            em.op("vector",
                                  lambda e, asl=asl, l=l_ap: e.reciprocal(
                                      rl[:, asl, :], l),
                                  reads=[lbuf], writes=[rl_b[asl]])
                            em.op("vector",
                                  lambda e, asl=asl: e.tensor_scalar_mul(
                                      attn[:, asl, :], ps_av[:, asl, 0:64],
                                      rl[:, asl, :]),
                                  reads=[ps_av_b[asl], rl_b[asl]],
                                  writes=[attn_b[asl]])
                            em.op("tensor",
                                  lambda e, asl=asl: e.transpose(
                                      ps_ao[:, asl, 0:128], attn[:, asl, :],
                                      ident[:]),
                                  reads=[attn_b[asl], ident_b],
                                  writes=[ps_ao_b[asl]])
                            em.op("vector",
                                  lambda e, asl=asl, base=base, ch=ch, c0=c0:
                                  e.tensor_copy(
                                      aoT[base:base + 64, ch, c0:c0 + 128],
                                      ps_ao[:, asl, 0:128]),
                                  reads=[ps_ao_b[asl]], writes=[aoT_b])
                    dst = ow_h.rearrange("h t p c -> t p h c")[T]
                    em.op("scalar",
                          lambda e, sl=sl, dst=dst: e.dma_start(
                              dst, owst[:, sl, :, :]),
                          reads=[owst_b[sl]], dma=True)
                if dbg:
                    em.op("scalar", lambda e: e.dma_start(dbg_ao_h, aoT[:]),
                          reads=[aoT_b], dma=True)
            em.run_phase("A2a")
            if upto == "a2a":
                return nc, em

            em.begin_phase()
            with contextlib.ExitStack() as ph:
                ec = ph.enter_context
                sq = ec(nc.sbuf_tensor("sq", [64, 2, S], BF16))
                sk = ec(nc.sbuf_tensor("sk", [64, 2, S], BF16))
                sv = ec(nc.sbuf_tensor("sv", [64, 2, S], BF16))
                owh = ec(nc.sbuf_tensor("owh", [128, 2, NT, 66], F32))
                opt = ec(nc.sbuf_tensor("opt", [66, 2, S], F32))
                st2 = ec(nc.sbuf_tensor("st2", [64, 2, 4], F32))
                sph = ec(nc.sbuf_tensor("sph", [64, 2, 128], F32))
                pph = ec(nc.sbuf_tensor("pph", [64, 2, 128], BF16))
                ptvp = ec(nc.sbuf_tensor("ptvp", [64, 2, 256], BF16))
                mph2 = ec(nc.sbuf_tensor("mph2", [64, 128], F32))
                mg_ab = ec(nc.sbuf_tensor("mg_ab", [128, 2, 2], F32))
                mg_d = ec(nc.sbuf_tensor("mg_d", [128, 2, 2], F32))
                mg_nm = ec(nc.sbuf_tensor("mg_nm", [128, 2, 1], F32))
                mg_l = ec(nc.sbuf_tensor("mg_l", [128, 2, 1], F32))
                rl2 = ec(nc.sbuf_tensor("rl2", [128, 2, 1], F32))
                mg_t = ec(nc.sbuf_tensor("mg_t", [128, 2, 64], F32))
                mg_o = ec(nc.sbuf_tensor("mg_o", [128, 2, 64], F32))
                attnb = ec(nc.sbuf_tensor("attnb", [128, 2, 64], BF16))
                tok_sb = ec(nc.sbuf_tensor("tok_sb", [128, 2, 66], F32))
                ps_ph = ec(nc.psum_tensor("ps_ph", [64, 2, 512], F32))
                ps_sm = ec(nc.psum_tensor("ps_sm", [64, 2, 1024], BF16))
                ps_opt = ec(nc.psum_tensor("ps_opt", [66, 2, 512], F32))
                ps_tok = ec(nc.psum_tensor("ps_tok", [64, 512], F32))
                ps_aob = ec(nc.psum_tensor("ps_aob", [64, 1024], BF16))
                sq_b = [Buf("sq0"), Buf("sq1")]
                sk_b = [Buf("sk0"), Buf("sk1")]
                sv_b = [Buf("sv0"), Buf("sv1")]
                owh_b = [Buf("oh0"), Buf("oh1")]
                opt_b = [Buf("opta"), Buf("optb")]
                st2_b = [Buf("st20"), Buf("st21")]
                sph_b = [Buf("sp0"), Buf("sp1")]
                pph_b = [Buf("pp0"), Buf("pp1")]
                ptvp_b = [Buf("pv0"), Buf("pv1")]
                mph2_b = Buf("mph2")
                ab_b = [Buf("ab0"), Buf("ab1")]
                d_b = [Buf("d0"), Buf("d1")]
                nm_b = [Buf("nm0"), Buf("nm1")]
                l_b = [Buf("l0"), Buf("l1")]
                rl2_b = [Buf("rlb0"), Buf("rlb1")]
                t_b = [Buf("t0"), Buf("t1")]
                o_b = [Buf("o0"), Buf("o1")]
                attnb_b = [Buf("atb0"), Buf("atb1")]
                tok_sb_b = [Buf("tk0"), Buf("tk1")]
                ps_ph_b = [Buf("pph0"), Buf("pph1")]
                ps_sm_b = [Buf("psm0"), Buf("psm1")]
                ps_opt_b = [Buf("po0"), Buf("po1")]
                ps_tok_b = Buf("ptk")
                ps_aob_b = Buf("paob")

                em.op("sync", lambda e: e.dma_start(mph2[:], mph2_h),
                      writes=[mph2_b], dma=True)
                for h in range(8):
                    sl = h % 2
                    em.op("sync", lambda e, h=h, sl=sl: e.dma_start(
                        sq[:, sl, :], qT_h[h * 64:(h + 1) * 64, :]),
                          writes=[sq_b[sl]], dma=True)
                    em.op("sync", lambda e, h=h, sl=sl: e.dma_start(
                        sk[:, sl, :], kT_h[h * 64:(h + 1) * 64, :]),
                          writes=[sk_b[sl]], dma=True)
                    em.op("sync", lambda e, h=h, sl=sl: e.dma_start(
                        sv[:, sl, :], vT_h[h * 64:(h + 1) * 64, :]),
                          writes=[sv_b[sl]], dma=True)
                    for T in range(NT):
                        em.op("sync", lambda e, h=h, sl=sl, T=T: e.dma_start(
                            owh[:, sl, T, :], ow_h[h, T]),
                              writes=[owh_b[sl]], dma=True)
                    qv = sq[:, sl, :].rearrange("d (b ph) -> d ph b", ph=64)
                    kv = sk[:, sl, :].rearrange("d (b ph) -> d ph b", ph=64)
                    vv = sv[:, sl, :].rearrange("d (b ph) -> d ph b", ph=64)
                    for pp in range(32):
                        p0 = 2 * pp
                        psl = pp % 2
                        def scmm2(e, psl=psl, p0=p0, qv=qv, kv=kv):
                            e.matmul(ps_ph[:, psl, 0:64], qv[:, p0, :],
                                     kv[:, p0, :], start=True, stop=True)
                            return e.matmul(ps_ph[:, psl, 64:128],
                                            qv[:, p0 + 1, :], kv[:, p0 + 1, :],
                                            start=True, stop=True)
                        em.op("tensor", scmm2, reads=[sq_b[sl], sk_b[sl]],
                              writes=[ps_ph_b[psl]])
                        em.op("vector",
                              lambda e, psl=psl: e.tensor_add(
                                  sph[:, psl, :], ps_ph[:, psl, 0:128],
                                  mph2[:]),
                              reads=[ps_ph_b[psl], mph2_b],
                              writes=[sph_b[psl]])
                        sphv = sph[:, psl, :].rearrange("p (q m) -> p q m",
                                                        m=64)
                        nm2 = st2[:, psl, :].rearrange("p (q t) -> p q t",
                                                       t=2)[:, :, 0]
                        em.op("vector",
                              lambda e, psl=psl, sphv=sphv, nm2=nm2:
                              e.reduce_max(nm2, sphv, axis=AX.X, negate=True),
                              reads=[sph_b[psl]], writes=[st2_b[psl]])
                        for q_ in range(2):
                            em.op("scalar",
                                  lambda e, psl=psl, q_=q_: e.activation(
                                      pph[:, psl, q_ * 64:(q_ + 1) * 64],
                                      sph[:, psl, q_ * 64:(q_ + 1) * 64],
                                      AF.Exp,
                                      bias=st2[:, psl, 2 * q_:2 * q_ + 1],
                                      accum_out=st2[:, psl,
                                                    2 * q_ + 1:2 * q_ + 2]),
                                  reads=[sph_b[psl], st2_b[psl]],
                                  writes=[pph_b[psl], st2_b[psl]])
                        def smtr2(e, psl=psl, p0=p0, vv=vv):
                            e.transpose(ps_sm[:, psl, 0:64],
                                        pph[:, psl, 0:64], ident[0:64, 0:64])
                            e.transpose(ps_sm[:, psl, 64:128],
                                        pph[:, psl, 64:128],
                                        ident[0:64, 0:64])
                            e.transpose(ps_sm[:, psl, 128:192],
                                        vv[:, p0, :], ident[0:64, 0:64])
                            return e.transpose(ps_sm[:, psl, 192:256],
                                               vv[:, p0 + 1, :],
                                               ident[0:64, 0:64])
                        em.op("tensor", smtr2,
                              reads=[pph_b[psl], sv_b[sl], ident_b],
                              writes=[ps_sm_b[psl]])
                        em.op("vector",
                              lambda e, psl=psl: e.tensor_copy(
                                  ptvp[:, psl, :], ps_sm[0:64, psl, 0:256]),
                              reads=[ps_sm_b[psl]], writes=[ptvp_b[psl]])
                        def optmm2(e, psl=psl):
                            e.matmul(ps_opt[0:64, psl, 0:64],
                                     ptvp[:, psl, 128:192],
                                     ptvp[:, psl, 0:64],
                                     start=True, stop=True)
                            e.matmul(ps_opt[0:64, psl, 64:128],
                                     ptvp[:, psl, 192:256],
                                     ptvp[:, psl, 64:128],
                                     start=True, stop=True)
                            e.transpose(ps_opt[0:2, psl, 256:320],
                                        st2[:, psl, 0:2],
                                        ident32[0:64, 0:64])
                            return e.transpose(ps_opt[0:2, psl, 320:384],
                                               st2[:, psl, 2:4],
                                               ident32[0:64, 0:64])
                        em.op("tensor", optmm2,
                              reads=[ptvp_b[psl], st2_b[psl], ident32_b],
                              writes=[ps_opt_b[psl]])
                        def opt_evac2(e, psl=psl, p0=p0, hsl=sl):
                            e.tensor_copy(
                                opt[0:64, hsl, p0 * 64:(p0 + 2) * 64],
                                ps_opt[0:64, psl, 0:128])
                            return e.tensor_copy(
                                opt[64:66, hsl, p0 * 64:(p0 + 2) * 64],
                                ps_opt[0:2, psl, 256:384])
                        em.op("vector", opt_evac2,
                              reads=[ps_opt_b[psl]], writes=[opt_b[sl]])
                    optv = opt.ap()[:, sl, :].rearrange("r (p bb) -> r p bb",
                                                        bb=64)
                    for T in range(NT):
                        tsl = T % 2
                        def toktr(e, T=T, optv=optv):
                            e.transpose(ps_tok[:, 0:66], optv[:, :, 2 * T],
                                        ident32[0:66, 0:66])
                            return e.transpose(ps_tok[:, 128:194],
                                               optv[:, :, 2 * T + 1],
                                               ident32[0:66, 0:66])
                        em.op("tensor", toktr, reads=[opt_b[sl], ident32_b],
                              writes=[ps_tok_b])
                        def tokas(e, tsl=tsl):
                            e.tensor_copy(tok_sb[0:64, tsl, :],
                                          ps_tok[:, 0:66])
                            return e.tensor_copy(tok_sb[64:128, tsl, :],
                                                 ps_tok[:, 128:194])
                        em.op("vector", tokas, reads=[ps_tok_b],
                              writes=[tok_sb_b[tsl]])
                        ow_o = owh[:, sl, T, 0:64]
                        ow_nm = owh[:, sl, T, 64:65]
                        ow_l = owh[:, sl, T, 65:66]
                        em.op("vector",
                              lambda e, tsl=tsl, ow_nm=ow_nm: e.tensor_tensor(
                                  mg_nm[:, tsl, :], ow_nm,
                                  tok_sb[:, tsl, 64:65], ALU.min),
                              reads=[owh_b[sl], tok_sb_b[tsl]],
                              writes=[nm_b[tsl]])
                        def dcalc(e, tsl=tsl, ow_nm=ow_nm):
                            e.tensor_tensor(mg_d[:, tsl, 0:1],
                                            mg_nm[:, tsl, :], ow_nm,
                                            ALU.subtract)
                            return e.tensor_tensor(
                                mg_d[:, tsl, 1:2], mg_nm[:, tsl, :],
                                tok_sb[:, tsl, 64:65], ALU.subtract)
                        em.op("vector", dcalc,
                              reads=[nm_b[tsl], owh_b[sl], tok_sb_b[tsl]],
                              writes=[d_b[tsl]])
                        em.op("scalar",
                              lambda e, tsl=tsl: e.activation(
                                  mg_ab[:, tsl, :], mg_d[:, tsl, :], AF.Exp),
                              reads=[d_b[tsl]], writes=[ab_b[tsl]])
                        em.op("vector",
                              lambda e, tsl=tsl, ow_o=ow_o: e.tensor_scalar_mul(
                                  mg_t[:, tsl, :], ow_o, mg_ab[:, tsl, 0:1]),
                              reads=[owh_b[sl], ab_b[tsl]], writes=[t_b[tsl]])
                        em.op("vector",
                              lambda e, tsl=tsl: e.scalar_tensor_tensor(
                                  mg_o[:, tsl, :], tok_sb[:, tsl, 0:64],
                                  mg_ab[:, tsl, 1:2], mg_t[:, tsl, :],
                                  ALU.mult, ALU.add),
                              reads=[tok_sb_b[tsl], ab_b[tsl], t_b[tsl]],
                              writes=[o_b[tsl]])
                        def lprod(e, tsl=tsl, ow_l=ow_l):
                            e.tensor_tensor(mg_d[:, tsl, 0:1], ow_l,
                                            mg_ab[:, tsl, 0:1], ALU.mult)
                            return e.tensor_tensor(mg_d[:, tsl, 1:2],
                                                   tok_sb[:, tsl, 65:66],
                                                   mg_ab[:, tsl, 1:2],
                                                   ALU.mult)
                        em.op("vector", lprod,
                              reads=[owh_b[sl], tok_sb_b[tsl], ab_b[tsl]],
                              writes=[d_b[tsl]])
                        em.op("vector",
                              lambda e, tsl=tsl: e.reduce_sum(
                                  mg_l[:, tsl, :], mg_d[:, tsl, :], axis=AX.X),
                              reads=[d_b[tsl]], writes=[l_b[tsl]])
                        em.op("vector",
                              lambda e, tsl=tsl: e.reciprocal(
                                  rl2[:, tsl, :], mg_l[:, tsl, :]),
                              reads=[l_b[tsl]], writes=[rl2_b[tsl]])
                        em.op("vector",
                              lambda e, tsl=tsl: e.tensor_scalar_mul(
                                  attnb[:, tsl, :], mg_o[:, tsl, :],
                                  rl2[:, tsl, :]),
                              reads=[o_b[tsl], rl2_b[tsl]],
                              writes=[attnb_b[tsl]])
                        em.op("tensor",
                              lambda e, tsl=tsl: e.transpose(
                                  ps_aob[:, 0:128], attnb[:, tsl, :],
                                  ident[:]),
                              reads=[attnb_b[tsl], ident_b],
                              writes=[ps_aob_b])
                        base = (h % 2) * 64
                        em.op("vector",
                              lambda e, base=base, h=h, T=T: e.tensor_copy(
                                  aoT[base:base + 64, h // 2,
                                      T * 128:(T + 1) * 128],
                                  ps_aob[:, 0:128]),
                              reads=[ps_aob_b], writes=[aoT_b])
                if dbg:
                    em.op("scalar", lambda e: e.dma_start(dbg_ao_h, aoT[:]),
                          reads=[aoT_b], dma=True)
            em.run_phase("A2b")
            if upto == "a2b":
                return nc, em

            em.begin_phase()
            with contextlib.ExitStack() as ph:
                ec = ph.enter_context
                wo_sb = ec(nc.sbuf_tensor("wo_sb", [128, 8, D], BF16))
                x_t3 = ec(nc.sbuf_tensor("x_t3", [128, 2, D], F32))
                ln2 = ec(nc.sbuf_tensor("ln2", [128, D], F32))
                x2t = ec(nc.sbuf_tensor("x2t", [128, 2, D], F32))
                ln3 = ec(nc.sbuf_tensor("ln3", [128, D], BF16))
                ln3Ts = ec(nc.sbuf_tensor("ln3Ts", [128, 2, 8, 128], BF16))
                ps_pr = ec(nc.psum_tensor("ps_pr", [128, 2, 512], F32))
                ps_tr3 = ec(nc.psum_tensor("ps_tr3", [128, 2, 1024], BF16))
                wo_b = Buf("wo")
                x_t3_b = [Buf("x30"), Buf("x31")]
                ln2_b = Buf("ln2")
                x2t_b = [Buf("x20"), Buf("x21")]
                ln3_b = Buf("ln3")
                ln3Ts_b = [Buf("l3T0"), Buf("l3T1")]
                ps_pr_b = [Buf("ppr0"), Buf("ppr1")]
                ps_tr3_b = [Buf("ptr30"), Buf("ptr31")]

                em.op("gpsimd",
                      lambda e: e.dma_start(
                          wo_sb[:], wo_h.rearrange("(c p) o -> p c o", p=128)),
                      writes=[wo_b], dma=True)
                for T in range(NT):
                    sl = T % 2
                    c0 = T * 128
                    em.op("sync",
                          lambda e, sl=sl, c0=c0: e.dma_start(
                              x_t3[:, sl, :], x_h[c0:c0 + 128, :]),
                          writes=[x_t3_b[sl]], dma=True)
                    for oh in range(2):
                        psl = oh
                        def prmm(e, psl=psl, oh=oh, c0=c0):
                            for fg in range(8):
                                ins = e.matmul(
                                    ps_pr[:, psl, :], aoT[:, fg, c0:c0 + 128],
                                    wo_sb[:, fg, oh * 512:(oh + 1) * 512],
                                    start=(fg == 0), stop=(fg == 7))
                            return ins
                        em.op("tensor", prmm, reads=[wo_b, aoT_b],
                              writes=[ps_pr_b[psl]])
                        em.op("vector",
                              lambda e, psl=psl, oh=oh: e.tensor_copy(
                                  ln2[:, oh * 512:(oh + 1) * 512],
                                  ps_pr[:, psl, :]),
                              reads=[ps_pr_b[psl]], writes=[ln2_b])
                    rstd, negmr, rstd_b, negmr_b = ln_stats(ln2[:], [ln2_b])
                    em.op("scalar",
                          lambda e, r=rstd, m=negmr: e.activation(
                              ln2[:], ln2[:], AF.Identity, bias=m, scale=r),
                          reads=[ln2_b, rstd_b, negmr_b], writes=[ln2_b])
                    em.op("vector",
                          lambda e, sl=sl: e.tensor_add(
                              x2t[:, sl, :], x_t3[:, sl, :], ln2[:]),
                          reads=[x_t3_b[sl], ln2_b], writes=[x2t_b[sl]])
                    em.op("scalar",
                          lambda e, sl=sl, c0=c0: e.dma_start(
                              x2_h[c0:c0 + 128, :], x2t[:, sl, :]),
                          reads=[x2t_b[sl]], dma=True)
                    rstd, negmr, rstd_b, negmr_b = ln_stats(
                        x2t[:, sl, :], [x2t_b[sl]])
                    em.op("scalar",
                          lambda e, sl=sl, r=rstd, m=negmr: e.activation(
                              ln3[:], x2t[:, sl, :], AF.Identity, bias=m,
                              scale=r),
                          reads=[x2t_b[sl], rstd_b, negmr_b], writes=[ln3_b])
                    for f in range(8):
                        psl = f % 2
                        em.op("tensor",
                              lambda e, f=f, psl=psl: e.transpose(
                                  ps_tr3[:, psl, 0:128],
                                  ln3[:, f * 128:(f + 1) * 128], ident[:]),
                              reads=[ln3_b, ident_b], writes=[ps_tr3_b[psl]])
                        em.op("vector",
                              lambda e, f=f, sl=sl, psl=psl: e.tensor_copy(
                                  ln3Ts[:, sl, f, :], ps_tr3[:, psl, 0:128]),
                              reads=[ps_tr3_b[psl]], writes=[ln3Ts_b[sl]])
                    em.op("scalar",
                          lambda e, sl=sl, c0=c0: e.dma_start(
                              ln3T_hv[:, :, c0:c0 + 128], ln3Ts[:, sl, :, :]),
                          reads=[ln3Ts_b[sl]], dma=True)
            em.run_phase("A3")
            if upto == "a3":
                return nc, em

        em.begin_phase()
        with contextlib.ExitStack() as ph:
            ec = ph.enter_context
            s1w_c = ec(nc.sbuf_tensor("s1w_c", [128, 2, FH, 128], BF16))
            s2w_c = ec(nc.sbuf_tensor("s2w_c", [128, 2, FH, 4, 128], BF16))
            acc = ec(nc.sbuf_tensor("acc", [128, 16, D], F32))
            ln3c = ec(nc.sbuf_tensor("ln3c", [128, 2, 2048], BF16))
            wcp = ec(nc.sbuf_tensor("wcp", [128, 2, 16, D], BF16))
            ybf = ec(nc.sbuf_tensor("ybf", [128, 2, 4, 512], BF16))
            g_s = ec(nc.sbuf_tensor("g_s", [128, 2, 512], BF16))
            hT = ec(nc.sbuf_tensor("hT", [128, 2, 16, 512], BF16))
            x2b = ec(nc.sbuf_tensor("x2b", [128, 2, D], F32))
            lnf = ec(nc.sbuf_tensor("lnf", [128, D], F32))
            ps_o = ec(nc.psum_tensor("ps_o", [128, 2, 1024], F32))
            ps_y = ec(nc.psum_tensor("ps_y", [128, 512], F32))
            ps_z = ec(nc.psum_tensor("ps_z", [128, 2, 512], F32))

            s1w_b = [Buf("s1wa"), Buf("s1wb")]
            s2w_b = [Buf("s2wa"), Buf("s2wb")]
            acc_b = [Buf("acc%d" % i) for i in range(16)]
            ln3c_b = [Buf("l3c0"), Buf("l3c1")]
            wcp_b = [Buf("wcpa"), Buf("wcpb")]
            ybf_b = [Buf("ybf0"), Buf("ybf1")]
            g_s_b = [Buf("gs0"), Buf("gs1")]
            hT_b = [Buf("hT0"), Buf("hT1")]
            x2b_b = [Buf("x2b0"), Buf("x2b1")]
            lnf_b = Buf("lnf")
            ps_o_b = [Buf("pso0"), Buf("pso1")]
            ps_y_b = Buf("psy")
            ps_z_b = [Buf("psz0"), Buf("psz1")]

            wc_hv = wc_h.rearrange("(pc p) o -> pc p o", p=128)
            s1src = s1w_h.rearrange("h c (pp p) m -> c p h (pp m)", p=128)
            s2src = s2w_h.rearrange("h c q (pp p) m -> c h p q (pp m)", p=128)
            def chunk_loads(cidx):
                half_, c_ = cidx // NCHUNK, cidx % NCHUNK
                cset_ = cidx % 2
                t0h_ = half_ * 2048
                em.op("sync",
                      lambda e, c=c_, cset=cset_, t0h=t0h_: e.dma_start(
                          ln3c[:, cset, :], ln3T_hv[:, c, t0h:t0h + 2048]),
                      writes=[ln3c_b[cset_]], dma=True)
                em.op("gpsimd",
                      lambda e, c=c_, cset=cset_: e.dma_start(
                          s1w_c[:, cset, :, :], s1src[c]),
                      writes=[s1w_b[cset_]], dma=True)
                for fh in range(FH):
                    em.op("gpsimd",
                          lambda e, c=c_, cset=cset_, fh=fh: e.dma_start(
                              s2w_c[:, cset, fh, :, :], s2src[c, fh]),
                          writes=[s2w_b[cset_]], dma=True)
                for p in range(16):
                    em.op("gpsimd",
                          lambda e, c=c_, cset=cset_, p=p: e.dma_start(
                              wcp[:, cset, p, :], wc_hv[c * 16 + p]),
                          writes=[wcp_b[cset_]], dma=True)

            chunk_loads(0)
            for half in range(2):
                t0h = half * 2048
                for c in range(NCHUNK):
                    cidx = half * NCHUNK + c
                    cset = cidx % 2
                    for st_ in ("tensor", "vector", "scalar"):
                        em.wait_only(st_, [("qG", em.cnt["qG"]),
                                           ("qY", em.cnt["qY"])])
                    if cidx + 1 < 16:
                        chunk_loads(cidx + 1)

                    for G in range(4):
                        gsl = G % 2
                        g0 = G * 512
                        for fh in range(4):
                            em.op("tensor",
                                  lambda e, cset=cset, g0=g0, fh=fh: e.matmul(
                                      ps_y[:], s1w_c[:, cset, fh, :],
                                      ln3c[:, cset, g0:g0 + 512],
                                      start=True, stop=True),
                                  reads=[s1w_b[cset], ln3c_b[cset]],
                                  writes=[ps_y_b])
                            em.op("vector",
                                  lambda e, gsl=gsl, fh=fh: e.tensor_copy(
                                      ybf[:, gsl, fh, :], ps_y[:]),
                                  reads=[ps_y_b], writes=[ybf_b[gsl]])
                        for fh in range(4):
                            for q in range(4):
                                piece = fh * 4 + q
                                zsl = piece % 2
                                em.op("tensor",
                                      lambda e, cset=cset, gsl=gsl, fh=fh, q=q,
                                      zsl=zsl: e.matmul(
                                          ps_z[:, zsl, :],
                                          s2w_c[:, cset, fh, q, :],
                                          ybf[:, gsl, fh, :],
                                          start=True, stop=True),
                                      reads=[s2w_b[cset], ybf_b[gsl]],
                                      writes=[ps_z_b[zsl]])
                                em.op("scalar",
                                      lambda e, zsl=zsl:
                                      e.activation(g_s[:, zsl, :],
                                                   ps_z[:, zsl, :], AF.Gelu),
                                      reads=[ps_z_b[zsl]],
                                      writes=[g_s_b[zsl]])
                                em.op("vector",
                                      lambda e, gsl=gsl, piece=piece, zsl=zsl:
                                      e.tensor_copy(hT[:, gsl, piece, :],
                                                    g_s[:, zsl, :]),
                                      reads=[g_s_b[zsl]], writes=[hT_b[gsl]])
                        for ti in range(4):
                            t = G * 4 + ti
                            tsl = t % 2
                            def cpmm(e, gsl=gsl, cset=cset, ti=ti, tsl=tsl):
                                for p in range(16):
                                    for oh in range(2):
                                        ins = e.matmul(
                                            ps_o[:, tsl,
                                                 oh * 512:(oh + 1) * 512],
                                            hT[:, gsl, p,
                                               ti * 128:(ti + 1) * 128],
                                            wcp[:, cset, p,
                                                oh * 512:(oh + 1) * 512],
                                            start=(p == 0), stop=(p == 15))
                                return ins
                            em.op("tensor", cpmm,
                                  reads=[hT_b[gsl], wcp_b[cset]],
                                  writes=[ps_o_b[tsl]])
                            if c == 0:
                                em.op("vector",
                                      lambda e, tsl=tsl, t=t: e.tensor_copy(
                                          acc[:, t, :], ps_o[:, tsl, :]),
                                      reads=[ps_o_b[tsl]], writes=[acc_b[t]])
                            else:
                                em.op("vector",
                                      lambda e, tsl=tsl, t=t: e.tensor_tensor(
                                          acc[:, t, :], acc[:, t, :],
                                          ps_o[:, tsl, :], ALU.add),
                                      reads=[ps_o_b[tsl], acc_b[t]],
                                      writes=[acc_b[t]])
                if dbg:
                    em.op("scalar",
                          lambda e, half=half: e.dma_start(
                              dbg_acc_h[half], acc[:]),
                          reads=list(acc_b), dma=True)
                for t in range(16):
                    tt = t0h + t * 128
                    sl = t % 2
                    em.op("sync",
                          lambda e, sl=sl, tt=tt: e.dma_start(
                              x2b[:, sl, :], x2_h[tt:tt + 128, :]),
                          writes=[x2b_b[sl]], dma=True)
                    rstd, negmr, rstd_b, negmr_b = ln_stats(
                        acc[:, t, :], [acc_b[t]])
                    em.op("scalar",
                          lambda e, t=t, r=rstd, m=negmr: e.activation(
                              lnf[:], acc[:, t, :], AF.Identity, bias=m,
                              scale=r),
                          reads=[acc_b[t], rstd_b, negmr_b], writes=[lnf_b])
                    em.op("vector",
                          lambda e, sl=sl: e.tensor_add(
                              x2b[:, sl, :], x2b[:, sl, :], lnf[:]),
                          reads=[x2b_b[sl], lnf_b], writes=[x2b_b[sl]])
                    em.op("scalar",
                          lambda e, sl=sl, tt=tt: e.dma_start(
                              y_h[tt:tt + 128, :], x2b[:, sl, :]),
                          reads=[x2b_b[sl]], dma=True)
        em.run_phase("B")
    return nc, em



def _build_masks():
    r = np.arange(128)[:, None]
    c = np.arange(192)[None, :]
    m_win = np.where((c >= r) & (c <= r + 64), 0.0, NEG).astype(np.float32)
    c0 = np.arange(128)[None, :]
    m_win0 = np.where((c0 <= r) & (c0 >= r - 64), 0.0, NEG).astype(np.float32)
    m_fix = np.full((NT, 128, 192), NEG, dtype=np.float32)
    diag = (c0 // 64 == r // 64) & (c0 <= r)
    m = np.arange(64)[None, :]
    for T in range(NT):
        m_fix[T, :, 0:128] = np.where(diag, 0.0, NEG)
        m_fix[T, :, 128:192] = np.where(m <= (2 * T + r // 64 - 1), 0.0, NEG)
    b = np.arange(64)[:, None]
    m_ph = np.where(m <= b - 2, 0.0, NEG).astype(np.float32)
    return m_win, m_win0, m_fix, m_ph


def _prep_inputs(L0_attn_x, L0_attn_Wq, L0_attn_Wk, L0_attn_Wv, L0_attn_Wo,
                 L0_ffn_rail_maker, L0_ffn_rail_up_proj, L0_ffn_w_cproj):
    x8 = np.ascontiguousarray(np.asarray(L0_attn_x, dtype=np.float32))
    wqT = np.ascontiguousarray(np.asarray(L0_attn_Wq, dtype=np.float32).T)
    wkT = np.ascontiguousarray(np.asarray(L0_attn_Wk, dtype=np.float32).T)
    wvT = np.ascontiguousarray(np.asarray(L0_attn_Wv, dtype=np.float32).T)
    woT = np.ascontiguousarray(np.asarray(L0_attn_Wo, dtype=np.float32).T)
    rm = np.asarray(L0_ffn_rail_maker, dtype=np.float32)
    rup = np.asarray(L0_ffn_rail_up_proj, dtype=np.float32)
    wc = np.asarray(L0_ffn_w_cproj, dtype=np.float32)

    s1w = np.zeros((FH, NCHUNK, 128, 128), np.float32)
    rmg = rm.reshape(FH, NCHUNK, 8, 16, 16)
    for rp in range(8):
        s1w[:, :, rp * 16:(rp + 1) * 16, rp * 16:(rp + 1) * 16] = rmg[:, :, rp]
    s2w = np.zeros((FH, NCHUNK, 4, 128, 128), np.float32)
    rupg = rup.reshape(FH, NCHUNK, 8, 4, 16, 16)
    for rp in range(8):
        s2w[:, :, :, rp * 16:(rp + 1) * 16, rp * 16:(rp + 1) * 16] = \
            rupg[:, :, rp].transpose(0, 1, 2, 4, 3)

    wct = np.ascontiguousarray(wc.T)
    R = np.arange(FF)
    cch = R // 2048
    rem = R % 2048
    piece = rem // 128
    mm_ = rem % 128
    hh = piece // 4
    qq = piece % 4
    rr = 8 * cch + mm_ // 16
    kk = 16 * qq + mm_ % 16
    wcT = np.ascontiguousarray(wct[hh * 4096 + rr * 64 + kk])

    m_win, m_win0, m_fix, m_ph = _build_masks()
    ident = np.eye(128, dtype=ml_dtypes.bfloat16)
    shared = dict(wqT=wqT, wkT=wkT, wvT=wvT, woT=woT, s1w=s1w, s2w=s2w,
                  wcT=wcT, m_win=m_win, m_win0=m_win0, m_fix=m_fix,
                  m_ph=m_ph,
                  m_ph2=np.ascontiguousarray(np.concatenate([m_ph, m_ph], 1)),
                  ident_c=ident)
    return x8, shared


def kernel(L0_attn_x, L0_attn_Wq, L0_attn_Wk, L0_attn_Wv, L0_attn_Wo,
           L0_ffn_rail_maker, L0_ffn_rail_up_proj, L0_ffn_w_cproj):
    x8, shared = _prep_inputs(L0_attn_x, L0_attn_Wq, L0_attn_Wk, L0_attn_Wv,
                              L0_attn_Wo, L0_ffn_rail_maker,
                              L0_ffn_rail_up_proj, L0_ffn_w_cproj)
    nc, _ = build_program("all")
    in_maps = [dict(shared, x=np.ascontiguousarray(x8[i])) for i in range(8)]
    res = run_bass_kernel_spmd(nc, in_maps, core_ids=list(range(8)))
    out = np.stack([res.results[i]["y"] for i in range(8)], axis=0)
    return out.astype(np.float32)


# version 08 (final)
# speedup vs baseline: 1.3223x; 1.0508x over previous
import contextlib
import numpy as np
import ml_dtypes
import concourse.bass as bass
import concourse.mybir as mybir
from concourse.bass_utils import run_bass_kernel_spmd

F32 = mybir.dt.float32
BF16 = mybir.dt.bfloat16
AF = mybir.ActivationFunctionType
ALU = mybir.AluOpType
AX = mybir.AxisListType

S = 4096
D = 1024
NT = 32
NH = 16
NEG = -1.0e30
EPS = 1e-5
QSCALE = 1.0 / 8.0

FH = 4
NCHUNK = 8
FF = 16384



STREAMS = ("tensor", "vector", "scalar", "gpsimd", "sync")
SEM_OF = {"tensor": "T", "vector": "V", "scalar": "S", "gpsimd": "G"}
DMA_SEM_OF = {"sync": "qY", "gpsimd": "qG", "scalar": "qS"}


class Buf:
    __slots__ = ("name", "w", "r")

    def __init__(self, name):
        self.name = name
        self.w = []
        self.r = []


class Em:
    def __init__(self, nc):
        self.nc = nc
        self.sem_names = ["T", "V", "S", "G", "qY", "qG", "qS"]
        self.sems = {}
        self.cnt = {n: 0 for n in self.sem_names}
        self.last_wait = {s: {} for s in STREAMS}
        self.thunks = None
        self.nops = 0

    def begin_phase(self):
        self.thunks = {s: [] for s in STREAMS}
        for s in STREAMS:
            self.last_wait[s] = dict(self.cnt)

    def _waits_for(self, stream, deps):
        need = {}
        for (sem, val) in deps:
            if stream == "tensor" and sem == "T":
                continue
            if need.get(sem, 0) < val:
                need[sem] = val
        lw = self.last_wait[stream]
        out = []
        for sem, val in need.items():
            if lw.get(sem, 0) >= val:
                continue
            lw[sem] = val
            out.append((sem, val))
        return out

    def op(self, stream, fn, reads=(), writes=(), dma=False):
        deps = []
        for b in reads:
            deps += b.w
        for b in writes:
            deps += b.w
            deps += b.r
        waits = self._waits_for(stream, deps)
        if dma:
            sem, inc = DMA_SEM_OF[stream], 16
        else:
            sem, inc = SEM_OF[stream], 1
        self.cnt[sem] += inc
        token = (sem, self.cnt[sem])
        sems = self.sems

        def thunk(e, waits=waits, fn=fn, sem=sem, inc=inc):
            for (s_, v_) in waits:
                e.wait_ge(sems[s_], v_)
            fn(e).then_inc(sems[sem], inc)

        self.thunks[stream].append(thunk)
        self.nops += 1
        for b in writes:
            b.w = [token]
            b.r = []
        for b in reads:
            b.r.append(token)
        return token

    def wait_only(self, stream, deps):
        waits = self._waits_for(stream, deps)
        if not waits:
            return
        sems = self.sems

        def thunk(e, waits=waits):
            for (s_, v_) in waits:
                e.wait_ge(sems[s_], v_)

        self.thunks[stream].append(thunk)

    def run_phase(self, name):
        for stream, sem in DMA_SEM_OF.items():
            self.wait_only(stream, [(sem, self.cnt[sem])])
        th = self.thunks
        with self.nc.named_scope(name), self.nc.Block(name) as block:
            if th["sync"]:
                @block.sync
                def _(e, th=th):
                    for t in th["sync"]:
                        t(e)
            if th["gpsimd"]:
                @block.gpsimd
                def _(e, th=th):
                    for t in th["gpsimd"]:
                        t(e)
            if th["tensor"]:
                @block.tensor
                def _(e, th=th):
                    for t in th["tensor"]:
                        t(e)
            if th["vector"]:
                @block.vector
                def _(e, th=th):
                    for t in th["vector"]:
                        t(e)
            if th["scalar"]:
                @block.scalar
                def _(e, th=th):
                    for t in th["scalar"]:
                        t(e)
        self.thunks = None



def build_program(upto="all", dbg=False):
    nc = bass.Bass("TRN2", target_bir_lowering=False)
    em = Em(nc)
    KIND = "ExternalOutput" if dbg else "Internal"

    x_h = nc.dram_tensor("x", [S, D], F32, kind="ExternalInput").ap()
    wq_h = nc.dram_tensor("wqT", [D, D], F32, kind="ExternalInput").ap()
    wk_h = nc.dram_tensor("wkT", [D, D], F32, kind="ExternalInput").ap()
    wv_h = nc.dram_tensor("wvT", [D, D], F32, kind="ExternalInput").ap()
    wo_h = nc.dram_tensor("woT", [D, D], F32, kind="ExternalInput").ap()
    s1w_h = nc.dram_tensor("s1w", [FH, NCHUNK, 128, 128], F32,
                           kind="ExternalInput").ap()
    s2w_h = nc.dram_tensor("s2w", [FH, NCHUNK, 4, 128, 128], F32,
                           kind="ExternalInput").ap()
    wc_h = nc.dram_tensor("wcT", [FF, D], F32, kind="ExternalInput").ap()
    mwin2_h = nc.dram_tensor("m_win", [128, 2, 192], F32,
                             kind="ExternalInput").ap()
    mwin02_h = nc.dram_tensor("m_win0", [128, 2, 128], F32,
                              kind="ExternalInput").ap()
    mfix2_h = nc.dram_tensor("m_fix", [NT, 128, 2, 192], F32,
                             kind="ExternalInput").ap()
    mph_h = nc.dram_tensor("m_ph", [64, 64], F32, kind="ExternalInput").ap()
    mph2_h = nc.dram_tensor("m_ph2", [64, 128], F32, kind="ExternalInput").ap()
    ident_h = nc.dram_tensor("ident_c", [128, 128], BF16, kind="ExternalInput").ap()
    y_h = nc.dram_tensor("y", [S, D], F32, kind="ExternalOutput").ap()

    qT_h = nc.dram_tensor("qT_f", [D, S], BF16, kind=KIND).ap()
    kT_h = nc.dram_tensor("kT_f", [D, S], BF16, kind=KIND).ap()
    vT_h = nc.dram_tensor("vT_f", [D, S], BF16, kind=KIND).ap()
    vtok_h = nc.dram_tensor("vtok", [NT, 128, D], BF16, kind=KIND).ap()
    ow_h = nc.dram_tensor("ow", [8, NT, 128, 66], F32, kind=KIND).ap()
    ln3T_h = nc.dram_tensor("ln3T", [D, S], BF16, kind=KIND).ap()
    x2_h = nc.dram_tensor("x2", [S, D], F32, kind=KIND).ap()
    if dbg:
        dbg_ao_h = nc.dram_tensor("dbg_ao", [128, 8, S], BF16,
                                  kind="ExternalOutput").ap()
        dbg_opt_h = nc.dram_tensor("dbg_opt", [66, S], F32,
                                   kind="ExternalOutput").ap()
        dbg_tok_h = nc.dram_tensor("dbg_tok", [128, 66], F32,
                                   kind="ExternalOutput").ap()
        dbg_ml_h = nc.dram_tensor("dbg_ml", [128, 8], F32,
                                  kind="ExternalOutput").ap()
        dbg_acc_h = nc.dram_tensor("dbg_acc", [2, 128, 16, D], F32,
                                   kind="ExternalOutput").ap()

    qT_hv = qT_h.rearrange("(c p) t -> p c t", p=128)
    kT_hv = kT_h.rearrange("(c p) t -> p c t", p=128)
    ln3T_hv = ln3T_h.rearrange("(c p) t -> p c t", p=128)

    stack = contextlib.ExitStack()
    with stack:
        for n in em.sem_names:
            em.sems[n] = stack.enter_context(nc.semaphore("sem_" + n))

        ident = stack.enter_context(nc.sbuf_tensor("ident", [128, 128], BF16))
        ident32 = stack.enter_context(nc.sbuf_tensor("ident32", [128, 128], F32))
        ident_b, ident32_b = Buf("ident"), Buf("ident32")
        st_t = {}
        st_b = {}
        for n in ("sums", "sumsq", "t_m", "t_var", "t_sd", "t_rstd", "t_negmr"):
            st_t[n] = stack.enter_context(nc.sbuf_tensor("st_" + n, [128, 1], F32))
            st_b[n] = Buf(n)
        scr = stack.enter_context(nc.sbuf_tensor("scr", [128, D], BF16))
        scr_b = Buf("scr")

        def ln_stats(src_ap, src_bufs, W=D):
            inv = 1.0 / W
            em.op("scalar",
                  lambda e: e.activation(scr[:, 0:W], src_ap, AF.Square,
                                         accum_out=st_t["sumsq"][:]),
                  reads=src_bufs, writes=[st_b["sumsq"], scr_b])
            em.op("vector",
                  lambda e: e.reduce_sum(st_t["sums"][:], src_ap, axis=AX.X),
                  reads=src_bufs, writes=[st_b["sums"]])
            em.op("vector",
                  lambda e: e.tensor_scalar_mul(st_t["t_m"][:], st_t["sums"][:],
                                                inv),
                  reads=[st_b["sums"]], writes=[st_b["t_m"]])
            em.op("vector",
                  lambda e: e.tensor_scalar(
                      st_t["t_var"][:], st_t["t_m"][:], st_t["t_m"][:], -EPS,
                      op0=ALU.mult, op1=ALU.add),
                  reads=[st_b["t_m"]], writes=[st_b["t_var"]])
            em.op("vector",
                  lambda e: e.scalar_tensor_tensor(
                      st_t["t_var"][:], st_t["sumsq"][:], inv, st_t["t_var"][:],
                      ALU.mult, ALU.subtract),
                  reads=[st_b["sumsq"], st_b["t_var"]], writes=[st_b["t_var"]])
            em.op("scalar",
                  lambda e: e.activation(st_t["t_sd"][:], st_t["t_var"][:],
                                         AF.Sqrt),
                  reads=[st_b["t_var"]], writes=[st_b["t_sd"]])
            em.op("vector",
                  lambda e: e.reciprocal(st_t["t_rstd"][:], st_t["t_sd"][:]),
                  reads=[st_b["t_sd"]], writes=[st_b["t_rstd"]])
            em.op("vector",
                  lambda e: e.scalar_tensor_tensor(
                      st_t["t_negmr"][:], st_t["t_m"][:], -1.0,
                      st_t["t_rstd"][:], ALU.mult, ALU.mult),
                  reads=[st_b["t_m"], st_b["t_rstd"]], writes=[st_b["t_negmr"]])
            return (st_t["t_rstd"][:], st_t["t_negmr"][:],
                    st_b["t_rstd"], st_b["t_negmr"])

        mid = contextlib.ExitStack()
        with mid:
            aoT = mid.enter_context(nc.sbuf_tensor("aoT", [128, 8, S], BF16))
            aoT_b = Buf("aoT")

            em.begin_phase()
            with contextlib.ExitStack() as ph:
                ec = ph.enter_context
                wq_sb = ec(nc.sbuf_tensor("wq_sb", [128, 8, D], BF16))
                wk_sb = ec(nc.sbuf_tensor("wk_sb", [128, 8, D], BF16))
                wv_sb = ec(nc.sbuf_tensor("wv_sb", [128, 8, D], BF16))
                x_t = ec(nc.sbuf_tensor("x_t", [128, 2, D], F32))
                ln1 = ec(nc.sbuf_tensor("ln1", [128, D], BF16))
                ln1T = ec(nc.sbuf_tensor("ln1T", [128, 2, 8, 512], BF16))
                pstage = ec(nc.sbuf_tensor("pstage", [128, 2, 8, 512], BF16))
                vtstage = ec(nc.sbuf_tensor("vtstage", [128, 4, D], BF16))
                ps_tr1 = ec(nc.psum_tensor("ps_tr1", [128, 2, 1024], BF16))
                ps_mm1 = ec(nc.psum_tensor("ps_mm1", [128, 2, 512], F32))
                wq_b, wk_b, wv_b = Buf("wq"), Buf("wk"), Buf("wv")
                x_t_b = [Buf("x_t0"), Buf("x_t1")]
                ln1_b = Buf("ln1")
                ln1T_b = [Buf("l1T0"), Buf("l1T1")]
                pstage_b = [Buf("ps0"), Buf("ps1")]
                vtstage_b = [Buf("vt%d" % i) for i in range(4)]
                ps_tr1_b = [Buf("ptr0"), Buf("ptr1")]
                ps_mm1_b = [Buf("pmm0"), Buf("pmm1")]

                em.op("sync", lambda e: e.dma_start(ident[:], ident_h),
                      writes=[ident_b], dma=True)
                em.op("vector", lambda e: e.tensor_copy(ident32[:], ident[:]),
                      reads=[ident_b], writes=[ident32_b])
                for wsb, wb, wh in ((wq_sb, wq_b, wq_h), (wk_sb, wk_b, wk_h),
                                    (wv_sb, wv_b, wv_h)):
                    src = wh.rearrange("(c p) o -> p c o", p=128)
                    em.op("gpsimd", lambda e, d=wsb, s=src: e.dma_start(d[:], s),
                          writes=[wb], dma=True)

                for G in range(8):
                    gsl = G % 2
                    for ti in range(4):
                        T = G * 4 + ti
                        sl = T % 2
                        xa = x_t[:, sl, :]
                        em.op("sync",
                              lambda e, xa=xa, T=T: e.dma_start(
                                  xa, x_h[T * 128:(T + 1) * 128, :]),
                              writes=[x_t_b[sl]], dma=True)
                        rstd, negmr, rstd_b, negmr_b = ln_stats(xa, [x_t_b[sl]])
                        em.op("scalar",
                              lambda e, xa=xa, r=rstd, m=negmr: e.activation(
                                  ln1[:], xa, AF.Identity, bias=m, scale=r),
                              reads=[x_t_b[sl], rstd_b, negmr_b],
                              writes=[ln1_b])
                        for f in range(8):
                            psl = f % 2
                            em.op("tensor",
                                  lambda e, f=f, psl=psl: e.transpose(
                                      ps_tr1[:, psl, 0:128],
                                      ln1[:, f * 128:(f + 1) * 128], ident[:]),
                                  reads=[ln1_b, ident_b],
                                  writes=[ps_tr1_b[psl]])
                            em.op("vector",
                                  lambda e, f=f, ti=ti, psl=psl, gsl=gsl:
                                  e.tensor_copy(
                                      ln1T[:, gsl, f, ti * 128:(ti + 1) * 128],
                                      ps_tr1[:, psl, 0:128]),
                                  reads=[ps_tr1_b[psl]], writes=[ln1T_b[gsl]])
                    for pi, (wsb, wb, outdst) in enumerate(
                            ((wq_sb, wq_b, qT_hv), (wk_sb, wk_b, kT_hv),
                             (wv_sb, wv_b, None))):
                        sl = pi % 2
                        for og in range(8):
                            psl = og % 2
                            def mmgrp(e, wsb=wsb, og=og, psl=psl, gsl=gsl):
                                for fg in range(8):
                                    ins = e.matmul(
                                        ps_mm1[:, psl, :],
                                        wsb[:, fg, og * 128:(og + 1) * 128],
                                        ln1T[:, gsl, fg, :],
                                        start=(fg == 0), stop=(fg == 7))
                                return ins
                            em.op("tensor", mmgrp, reads=[wb, ln1T_b[gsl]],
                                  writes=[ps_mm1_b[psl]])
                            if pi == 0:
                                em.op("vector",
                                      lambda e, og=og, psl=psl, sl=sl:
                                      e.tensor_scalar_mul(
                                          pstage[:, sl, og, :],
                                          ps_mm1[:, psl, :], QSCALE),
                                      reads=[ps_mm1_b[psl]],
                                      writes=[pstage_b[sl]])
                            else:
                                em.op("vector",
                                      lambda e, og=og, psl=psl, sl=sl:
                                      e.tensor_copy(pstage[:, sl, og, :],
                                                    ps_mm1[:, psl, :]),
                                      reads=[ps_mm1_b[psl]],
                                      writes=[pstage_b[sl]])
                            if pi == 2:
                                for ti in range(4):
                                    T = G * 4 + ti
                                    tsl = (og * 4 + ti) % 2
                                    em.op("tensor",
                                          lambda e, og=og, ti=ti, sl=sl, tsl=tsl:
                                          e.transpose(
                                              ps_tr1[:, tsl, 0:128],
                                              pstage[:, sl, og,
                                                     ti * 128:(ti + 1) * 128],
                                              ident[:]),
                                          reads=[pstage_b[sl], ident_b],
                                          writes=[ps_tr1_b[tsl]])
                                    em.op("vector",
                                          lambda e, og=og, ti=ti, tsl=tsl:
                                          e.tensor_copy(
                                              vtstage[:, ti,
                                                      og * 128:(og + 1) * 128],
                                              ps_tr1[:, tsl, 0:128]),
                                          reads=[ps_tr1_b[tsl]],
                                          writes=[vtstage_b[ti]])
                        if pi < 2:
                            dst = outdst[:, :, G * 512:(G + 1) * 512]
                            em.op("scalar",
                                  lambda e, dst=dst, sl=sl: e.dma_start(
                                      dst, pstage[:, sl, :, :]),
                                  reads=[pstage_b[sl]], dma=True)
                        else:
                            dst = vT_h.rearrange("(c p) t -> p c t", p=128)[
                                :, :, G * 512:(G + 1) * 512]
                            em.op("scalar",
                                  lambda e, dst=dst, sl=sl: e.dma_start(
                                      dst, pstage[:, sl, :, :]),
                                  reads=[pstage_b[sl]], dma=True)
                            for ti in range(4):
                                T = G * 4 + ti
                                em.op("scalar",
                                      lambda e, T=T, ti=ti: e.dma_start(
                                          vtok_h[T], vtstage[:, ti, :]),
                                      reads=[vtstage_b[ti]], dma=True)
            em.run_phase("A1")
            if upto == "a1":
                return nc, em

            em.begin_phase()
            with contextlib.ExitStack() as ph:
                ec = ph.enter_context
                kwin = ec(nc.sbuf_tensor("kwin", [128, 2, 8, 192], BF16))
                qt = ec(nc.sbuf_tensor("qt", [128, 2, 8, 128], BF16))
                vprev = ec(nc.sbuf_tensor("vprev", [64, 2, D], BF16))
                vcur = ec(nc.sbuf_tensor("vcur", [128, 2, D], BF16))
                ksum = ec(nc.sbuf_tensor("ksum", [128, 4, 64], BF16))
                vsum = ec(nc.sbuf_tensor("vsum", [64, 8, 64], BF16))
                ptmp = ec(nc.sbuf_tensor("ptmp", [128, S], BF16))
                s_sb = ec(nc.sbuf_tensor("s_sb", [128, 2, 192], F32))
                p_sb = ec(nc.sbuf_tensor("p_sb", [128, 2, 192], BF16))
                ptsb = ec(nc.sbuf_tensor("ptsb", [128, 2, 256], BF16))
                owst = ec(nc.sbuf_tensor("owst", [128, 2, 8, 66], F32))
                attn = ec(nc.sbuf_tensor("attn", [128, 2, 64], BF16))
                rl = ec(nc.sbuf_tensor("rl", [128, 2, 1], F32))
                fxs = ec(nc.sbuf_tensor("fxs", [128, 2, 2], F32))
                mwin2 = ec(nc.sbuf_tensor("mwin2", [128, 2, 192], F32))
                mwin02 = ec(nc.sbuf_tensor("mwin02", [128, 2, 128], F32))
                mfix2 = ec(nc.sbuf_tensor("mfix2", [128, NT, 2, 192], F32))
                ps_sc = ec(nc.psum_tensor("ps_sc", [128, 2, 512], F32))
                ps_pt = ec(nc.psum_tensor("ps_pt", [128, 2, 1024], BF16))
                ps_av = ec(nc.psum_tensor("ps_av", [128, 2, 512], F32))
                ps_ao = ec(nc.psum_tensor("ps_ao", [64, 2, 1024], BF16))
                kwin_b = [Buf("kw0"), Buf("kw1")]
                qt_b = [Buf("qt0"), Buf("qt1")]
                vprev_b = [Buf("vp0"), Buf("vp1")]
                vcur_b = [Buf("vc0"), Buf("vc1")]
                ksum_b, vsum_b, ptmp_b = Buf("ksum"), Buf("vsum"), Buf("ptmp")
                s_b = [Buf("s0"), Buf("s1")]
                p_b = [Buf("p0"), Buf("p1")]
                ptsb_b = [Buf("pt0"), Buf("pt1")]
                owst_b = [Buf("ow0"), Buf("ow1")]
                attn_b = [Buf("at0"), Buf("at1")]
                rl_b = [Buf("rl0"), Buf("rl1")]
                fxs_b = [Buf("fx0"), Buf("fx1")]
                mwin2_b, mwin02_b, mfix2_b = Buf("mwin2"), Buf("mwin02"), Buf("mfix2")
                ps_sc_b = [Buf("psc0"), Buf("psc1")]
                ps_pt_b = [Buf("ppt0"), Buf("ppt1")]
                ps_av_b = [Buf("pav0"), Buf("pav1")]
                ps_ao_b = [Buf("pao0"), Buf("pao1")]

                em.op("sync", lambda e: e.dma_start(mwin2[:], mwin2_h),
                      writes=[mwin2_b], dma=True)
                em.op("sync", lambda e: e.dma_start(mwin02[:], mwin02_h),
                      writes=[mwin02_b], dma=True)
                em.op("sync", lambda e: e.dma_start(
                    mfix2[:], mfix2_h.rearrange("t p q c -> p t q c")),
                      writes=[mfix2_b], dma=True)
                for h in range(8, 16):
                    base = (h % 2) * 64
                    em.op("sync",
                          lambda e, h=h, base=base: e.dma_start(
                              ptmp[base:base + 64, :],
                              kT_h[h * 64:(h + 1) * 64, :]),
                          writes=[ptmp_b], dma=True)
                    kv = ptmp.ap().rearrange("p (m ph) -> p m ph", ph=64)
                    em.op("vector",
                          lambda e, h=h, base=base, kv=kv: e.tensor_copy(
                              ksum[base:base + 64, (h - 8) // 2, :],
                              kv[base:base + 64, :, 63]),
                          reads=[ptmp_b], writes=[ksum_b])
                for h in range(8, 16):
                    em.op("sync",
                          lambda e, h=h: e.dma_start(
                              ptmp[0:64, :], vT_h[h * 64:(h + 1) * 64, :]),
                          writes=[ptmp_b], dma=True)
                    vv = ptmp.ap().rearrange("p (m ph) -> p m ph", ph=64)
                    em.op("tensor",
                          lambda e, vv=vv: e.transpose(
                              ps_ao[0:64, 0, 0:64], vv[0:64, :, 63],
                              ident[0:64, 0:64]),
                          reads=[ptmp_b, ident_b], writes=[ps_ao_b[0]])
                    em.op("vector",
                          lambda e, h=h: e.tensor_copy(
                              vsum[:, h - 8, :], ps_ao[0:64, 0, 0:64]),
                          reads=[ps_ao_b[0]], writes=[vsum_b])

                for T in range(NT):
                    sl = T % 2
                    c0 = T * 128
                    em.op("sync",
                          lambda e, sl=sl, c0=c0: e.dma_start(
                              qt[:, sl, :, :], qT_hv[:, :, c0:c0 + 128]),
                          writes=[qt_b[sl]], dma=True)
                    if T == 0:
                        em.op("sync",
                              lambda e, sl=sl: e.dma_start(
                                  kwin[:, sl, :, 64:192], kT_hv[:, :, 0:128]),
                              writes=[kwin_b[sl]], dma=True)
                        em.op("sync",
                              lambda e, sl=sl: e.dma_start(
                                  vcur[:, sl, :], vtok_h[0]),
                              writes=[vcur_b[sl]], dma=True)
                    else:
                        em.op("sync",
                              lambda e, sl=sl, c0=c0: e.dma_start(
                                  kwin[:, sl, :, :],
                                  kT_hv[:, :, c0 - 64:c0 + 128]),
                              writes=[kwin_b[sl]], dma=True)
                        em.op("sync",
                              lambda e, sl=sl, T=T: e.dma_start(
                                  vprev[:, sl, :], vtok_h[T - 1, 64:128, :]),
                              writes=[vprev_b[sl]], dma=True)
                        em.op("sync",
                              lambda e, sl=sl, T=T: e.dma_start(
                                  vcur[:, sl, :], vtok_h[T]),
                              writes=[vcur_b[sl]], dma=True)

                    for k in range(8):
                        strided = k < 4
                        for j in range(2):
                            h = 2 * k + j
                            asl = j
                            base = (h % 2) * 64
                            ch = h // 2
                            qsl = qt[base:base + 64, sl, ch, :]
                            if strided:
                                if T == 0:
                                    em.op("tensor",
                                          lambda e, asl=asl, sl=sl, qsl=qsl,
                                          base=base, ch=ch:
                                          e.matmul(ps_sc[:, asl, 64:192], qsl,
                                                   kwin[base:base + 64, sl, ch,
                                                        64:192],
                                                   start=True, stop=True),
                                          reads=[qt_b[sl], kwin_b[sl]],
                                          writes=[ps_sc_b[asl]])
                                else:
                                    em.op("tensor",
                                          lambda e, asl=asl, sl=sl, qsl=qsl,
                                          base=base, ch=ch:
                                          e.matmul(ps_sc[:, asl, 0:192], qsl,
                                                   kwin[base:base + 64, sl,
                                                        ch, :],
                                                   start=True, stop=True),
                                          reads=[qt_b[sl], kwin_b[sl]],
                                          writes=[ps_sc_b[asl]])
                            else:
                                def scmm(e, asl=asl, sl=sl, qsl=qsl, base=base,
                                         ch=ch, h=h):
                                    e.matmul(ps_sc[:, asl, 0:128], qsl,
                                             kwin[base:base + 64, sl, ch,
                                                  64:192],
                                             start=True, stop=True)
                                    return e.matmul(
                                        ps_sc[:, asl, 128:192], qsl,
                                        ksum[base:base + 64, (h - 8) // 2, :],
                                        start=True, stop=True)
                                em.op("tensor", scmm,
                                      reads=[qt_b[sl], kwin_b[sl], ksum_b],
                                      writes=[ps_sc_b[asl]])
                        r0, r1 = (64, 192) if (strided and T == 0) else (0, 192)
                        if strided:
                            if T == 0:
                                mk, mk_b = mwin02[:, :, :], mwin02_b
                            else:
                                mk, mk_b = mwin2[:, :, :], mwin2_b
                        else:
                            mk, mk_b = mfix2[:, T, :, :], mfix2_b
                        em.op("vector",
                              lambda e, r0=r0, r1=r1, mk=mk: e.tensor_add(
                                  s_sb[:, 0:2, r0:r1], ps_sc[:, 0:2, r0:r1],
                                  mk),
                              reads=[ps_sc_b[0], ps_sc_b[1], mk_b],
                              writes=[s_b[0], s_b[1]])
                        if strided:
                            h0 = 2 * k
                            nm2 = owst[:, sl, h0:h0 + 2, 64:65]
                            nm_wbs = [owst_b[sl]]
                        else:
                            nm2 = fxs[:, 0:2, 0:1]
                            nm_wbs = [fxs_b[0], fxs_b[1]]
                        em.op("vector",
                              lambda e, r0=r0, r1=r1, nm2=nm2: e.reduce_max(
                                  nm2, s_sb[:, 0:2, r0:r1], axis=AX.X,
                                  negate=True),
                              reads=[s_b[0], s_b[1]], writes=nm_wbs)
                        for j in range(2):
                            h = 2 * k + j
                            asl = j
                            base = (h % 2) * 64
                            ch = h // 2
                            if strided:
                                nm_ap = owst[:, sl, h, 64:65]
                                l_ap = owst[:, sl, h, 65:66]
                                nm_rb = owst_b[sl]
                                lbuf = owst_b[sl]
                            else:
                                nm_ap = fxs[:, asl, 0:1]
                                l_ap = fxs[:, asl, 1:2]
                                nm_rb = fxs_b[asl]
                                lbuf = fxs_b[asl]
                            em.op("scalar",
                                  lambda e, asl=asl, r0=r0, r1=r1, nm=nm_ap,
                                  l=l_ap:
                                  e.activation(p_sb[:, asl, r0:r1],
                                               s_sb[:, asl, r0:r1], AF.Exp,
                                               bias=nm, accum_out=l),
                                  reads=[s_b[asl], nm_rb],
                                  writes=[p_b[asl], lbuf])
                            if strided:
                                def ptr(e, asl=asl, T=T):
                                    if T > 0:
                                        e.transpose(ps_pt[0:64, asl, 0:128],
                                                    p_sb[:, asl, 0:64],
                                                    ident[:])
                                    return e.transpose(ps_pt[:, asl, 128:256],
                                                       p_sb[:, asl, 64:192],
                                                       ident[:])
                            else:
                                def ptr(e, asl=asl):
                                    e.transpose(ps_pt[:, asl, 0:128],
                                                p_sb[:, asl, 0:128], ident[:])
                                    return e.transpose(
                                        ps_pt[0:64, asl, 128:256],
                                        p_sb[:, asl, 128:192], ident[:])
                            em.op("tensor", ptr, reads=[p_b[asl], ident_b],
                                  writes=[ps_pt_b[asl]])
                            em.op("vector",
                                  lambda e, asl=asl: e.tensor_copy(
                                      ptsb[:, asl, :], ps_pt[:, asl, 0:256]),
                                  reads=[ps_pt_b[asl]], writes=[ptsb_b[asl]])
                            if strided:
                                def avmm(e, asl=asl, sl=sl, h=h, T=T):
                                    ins = e.matmul(
                                        ps_av[:, asl, 0:64],
                                        ptsb[0:128, asl, 128:256],
                                        vcur[:, sl, h * 64:(h + 1) * 64],
                                        start=True, stop=(T == 0))
                                    if T > 0:
                                        ins = e.matmul(
                                            ps_av[:, asl, 0:64],
                                            ptsb[0:64, asl, 0:128],
                                            vprev[:, sl, h * 64:(h + 1) * 64],
                                            start=False, stop=True)
                                    return ins
                                em.op("tensor", avmm,
                                      reads=[ptsb_b[asl], vcur_b[sl]] +
                                            ([vprev_b[sl]] if T > 0 else []),
                                      writes=[ps_av_b[asl]])
                                em.op("vector",
                                      lambda e, asl=asl, sl=sl, h=h:
                                      e.tensor_copy(
                                          owst[:, sl, h, 0:64],
                                          ps_av[:, asl, 0:64]),
                                      reads=[ps_av_b[asl]],
                                      writes=[owst_b[sl]])
                            else:
                                def avmm(e, asl=asl, sl=sl, h=h):
                                    e.matmul(ps_av[:, asl, 0:64],
                                             ptsb[0:128, asl, 0:128],
                                             vcur[:, sl, h * 64:(h + 1) * 64],
                                             start=True, stop=False)
                                    return e.matmul(ps_av[:, asl, 0:64],
                                                    ptsb[0:64, asl, 128:256],
                                                    vsum[:, h - 8, :],
                                                    start=False, stop=True)
                                em.op("tensor", avmm,
                                      reads=[ptsb_b[asl], vcur_b[sl], vsum_b],
                                      writes=[ps_av_b[asl]])
                                em.op("vector",
                                      lambda e, asl=asl, l=l_ap: e.reciprocal(
                                          rl[:, asl, :], l),
                                      reads=[lbuf], writes=[rl_b[asl]])
                                em.op("vector",
                                      lambda e, asl=asl: e.tensor_scalar_mul(
                                          attn[:, asl, :],
                                          ps_av[:, asl, 0:64], rl[:, asl, :]),
                                      reads=[ps_av_b[asl], rl_b[asl]],
                                      writes=[attn_b[asl]])
                                em.op("tensor",
                                      lambda e, asl=asl: e.transpose(
                                          ps_ao[:, asl, 0:128],
                                          attn[:, asl, :], ident[:]),
                                      reads=[attn_b[asl], ident_b],
                                      writes=[ps_ao_b[asl]])
                                em.op("vector",
                                      lambda e, asl=asl, base=base, ch=ch,
                                      c0=c0:
                                      e.tensor_copy(
                                          aoT[base:base + 64, ch,
                                              c0:c0 + 128],
                                          ps_ao[:, asl, 0:128]),
                                      reads=[ps_ao_b[asl]], writes=[aoT_b])
                    dst = ow_h.rearrange("h t p c -> t p h c")[T]
                    em.op("scalar",
                          lambda e, sl=sl, dst=dst: e.dma_start(
                              dst, owst[:, sl, :, :]),
                          reads=[owst_b[sl]], dma=True)
                if dbg:
                    em.op("scalar", lambda e: e.dma_start(dbg_ao_h, aoT[:]),
                          reads=[aoT_b], dma=True)
            em.run_phase("A2a")
            if upto == "a2a":
                return nc, em

            em.begin_phase()
            with contextlib.ExitStack() as ph:
                ec = ph.enter_context
                sq = ec(nc.sbuf_tensor("sq", [64, 2, S], BF16))
                sk = ec(nc.sbuf_tensor("sk", [64, 2, S], BF16))
                sv = ec(nc.sbuf_tensor("sv", [64, 2, S], BF16))
                owh = ec(nc.sbuf_tensor("owh", [128, 2, NT, 66], F32))
                opt = ec(nc.sbuf_tensor("opt", [66, 2, S], F32))
                st2 = ec(nc.sbuf_tensor("st2", [64, 2, 4], F32))
                sph = ec(nc.sbuf_tensor("sph", [64, 2, 128], F32))
                pph = ec(nc.sbuf_tensor("pph", [64, 2, 128], BF16))
                ptvp = ec(nc.sbuf_tensor("ptvp", [64, 2, 256], BF16))
                mph2 = ec(nc.sbuf_tensor("mph2", [64, 128], F32))
                mg_ab = ec(nc.sbuf_tensor("mg_ab", [128, 2, 2], F32))
                mg_d = ec(nc.sbuf_tensor("mg_d", [128, 2, 2], F32))
                mg_nm = ec(nc.sbuf_tensor("mg_nm", [128, 2, 1], F32))
                mg_l = ec(nc.sbuf_tensor("mg_l", [128, 2, 1], F32))
                rl2 = ec(nc.sbuf_tensor("rl2", [128, 2, 1], F32))
                mg_t = ec(nc.sbuf_tensor("mg_t", [128, 2, 64], F32))
                mg_o = ec(nc.sbuf_tensor("mg_o", [128, 2, 64], F32))
                attnb = ec(nc.sbuf_tensor("attnb", [128, 2, 64], BF16))
                tok_sb = ec(nc.sbuf_tensor("tok_sb", [128, 2, 66], F32))
                ps_ph = ec(nc.psum_tensor("ps_ph", [64, 2, 512], F32))
                ps_sm = ec(nc.psum_tensor("ps_sm", [64, 2, 1024], BF16))
                ps_opt = ec(nc.psum_tensor("ps_opt", [66, 2, 512], F32))
                ps_tok = ec(nc.psum_tensor("ps_tok", [64, 512], F32))
                ps_aob = ec(nc.psum_tensor("ps_aob", [64, 1024], BF16))
                sq_b = [Buf("sq0"), Buf("sq1")]
                sk_b = [Buf("sk0"), Buf("sk1")]
                sv_b = [Buf("sv0"), Buf("sv1")]
                owh_b = [Buf("oh0"), Buf("oh1")]
                opt_b = [Buf("opta"), Buf("optb")]
                st2_b = [Buf("st20"), Buf("st21")]
                sph_b = [Buf("sp0"), Buf("sp1")]
                pph_b = [Buf("pp0"), Buf("pp1")]
                ptvp_b = [Buf("pv0"), Buf("pv1")]
                mph2_b = Buf("mph2")
                ab_b = [Buf("ab0"), Buf("ab1")]
                d_b = [Buf("d0"), Buf("d1")]
                nm_b = [Buf("nm0"), Buf("nm1")]
                l_b = [Buf("l0"), Buf("l1")]
                rl2_b = [Buf("rlb0"), Buf("rlb1")]
                t_b = [Buf("t0"), Buf("t1")]
                o_b = [Buf("o0"), Buf("o1")]
                attnb_b = [Buf("atb0"), Buf("atb1")]
                tok_sb_b = [Buf("tk0"), Buf("tk1")]
                ps_ph_b = [Buf("pph0"), Buf("pph1")]
                ps_sm_b = [Buf("psm0"), Buf("psm1")]
                ps_opt_b = [Buf("po0"), Buf("po1")]
                ps_tok_b = Buf("ptk")
                ps_aob_b = Buf("paob")

                em.op("sync", lambda e: e.dma_start(mph2[:], mph2_h),
                      writes=[mph2_b], dma=True)
                for h in range(8):
                    sl = h % 2
                    em.op("sync", lambda e, h=h, sl=sl: e.dma_start(
                        sq[:, sl, :], qT_h[h * 64:(h + 1) * 64, :]),
                          writes=[sq_b[sl]], dma=True)
                    em.op("sync", lambda e, h=h, sl=sl: e.dma_start(
                        sk[:, sl, :], kT_h[h * 64:(h + 1) * 64, :]),
                          writes=[sk_b[sl]], dma=True)
                    em.op("sync", lambda e, h=h, sl=sl: e.dma_start(
                        sv[:, sl, :], vT_h[h * 64:(h + 1) * 64, :]),
                          writes=[sv_b[sl]], dma=True)
                    for T in range(NT):
                        em.op("sync", lambda e, h=h, sl=sl, T=T: e.dma_start(
                            owh[:, sl, T, :], ow_h[h, T]),
                              writes=[owh_b[sl]], dma=True)
                    qv = sq[:, sl, :].rearrange("d (b ph) -> d ph b", ph=64)
                    kv = sk[:, sl, :].rearrange("d (b ph) -> d ph b", ph=64)
                    vv = sv[:, sl, :].rearrange("d (b ph) -> d ph b", ph=64)
                    for pp in range(32):
                        p0 = 2 * pp
                        psl = pp % 2
                        def scmm2(e, psl=psl, p0=p0, qv=qv, kv=kv):
                            e.matmul(ps_ph[:, psl, 0:64], qv[:, p0, :],
                                     kv[:, p0, :], start=True, stop=True)
                            return e.matmul(ps_ph[:, psl, 64:128],
                                            qv[:, p0 + 1, :], kv[:, p0 + 1, :],
                                            start=True, stop=True)
                        em.op("tensor", scmm2, reads=[sq_b[sl], sk_b[sl]],
                              writes=[ps_ph_b[psl]])
                        em.op("vector",
                              lambda e, psl=psl: e.tensor_add(
                                  sph[:, psl, :], ps_ph[:, psl, 0:128],
                                  mph2[:]),
                              reads=[ps_ph_b[psl], mph2_b],
                              writes=[sph_b[psl]])
                        sphv = sph[:, psl, :].rearrange("p (q m) -> p q m",
                                                        m=64)
                        nm2 = st2[:, psl, :].rearrange("p (q t) -> p q t",
                                                       t=2)[:, :, 0]
                        em.op("vector",
                              lambda e, psl=psl, sphv=sphv, nm2=nm2:
                              e.reduce_max(nm2, sphv, axis=AX.X, negate=True),
                              reads=[sph_b[psl]], writes=[st2_b[psl]])
                        for q_ in range(2):
                            em.op("scalar",
                                  lambda e, psl=psl, q_=q_: e.activation(
                                      pph[:, psl, q_ * 64:(q_ + 1) * 64],
                                      sph[:, psl, q_ * 64:(q_ + 1) * 64],
                                      AF.Exp,
                                      bias=st2[:, psl, 2 * q_:2 * q_ + 1],
                                      accum_out=st2[:, psl,
                                                    2 * q_ + 1:2 * q_ + 2]),
                                  reads=[sph_b[psl], st2_b[psl]],
                                  writes=[pph_b[psl], st2_b[psl]])
                        def smtr2(e, psl=psl, p0=p0, vv=vv):
                            e.transpose(ps_sm[:, psl, 0:64],
                                        pph[:, psl, 0:64], ident[0:64, 0:64])
                            e.transpose(ps_sm[:, psl, 64:128],
                                        pph[:, psl, 64:128],
                                        ident[0:64, 0:64])
                            e.transpose(ps_sm[:, psl, 128:192],
                                        vv[:, p0, :], ident[0:64, 0:64])
                            return e.transpose(ps_sm[:, psl, 192:256],
                                               vv[:, p0 + 1, :],
                                               ident[0:64, 0:64])
                        em.op("tensor", smtr2,
                              reads=[pph_b[psl], sv_b[sl], ident_b],
                              writes=[ps_sm_b[psl]])
                        em.op("vector",
                              lambda e, psl=psl: e.tensor_copy(
                                  ptvp[:, psl, :], ps_sm[0:64, psl, 0:256]),
                              reads=[ps_sm_b[psl]], writes=[ptvp_b[psl]])
                        def optmm2(e, psl=psl):
                            e.matmul(ps_opt[0:64, psl, 0:64],
                                     ptvp[:, psl, 128:192],
                                     ptvp[:, psl, 0:64],
                                     start=True, stop=True)
                            e.matmul(ps_opt[0:64, psl, 64:128],
                                     ptvp[:, psl, 192:256],
                                     ptvp[:, psl, 64:128],
                                     start=True, stop=True)
                            e.transpose(ps_opt[0:2, psl, 256:320],
                                        st2[:, psl, 0:2],
                                        ident32[0:64, 0:64])
                            return e.transpose(ps_opt[0:2, psl, 320:384],
                                               st2[:, psl, 2:4],
                                               ident32[0:64, 0:64])
                        em.op("tensor", optmm2,
                              reads=[ptvp_b[psl], st2_b[psl], ident32_b],
                              writes=[ps_opt_b[psl]])
                        def opt_evac2(e, psl=psl, p0=p0, hsl=sl):
                            e.tensor_copy(
                                opt[0:64, hsl, p0 * 64:(p0 + 2) * 64],
                                ps_opt[0:64, psl, 0:128])
                            return e.tensor_copy(
                                opt[64:66, hsl, p0 * 64:(p0 + 2) * 64],
                                ps_opt[0:2, psl, 256:384])
                        em.op("vector", opt_evac2,
                              reads=[ps_opt_b[psl]], writes=[opt_b[sl]])
                    optv = opt.ap()[:, sl, :].rearrange("r (p bb) -> r p bb",
                                                        bb=64)
                    for T in range(NT):
                        tsl = T % 2
                        def toktr(e, T=T, optv=optv):
                            e.transpose(ps_tok[:, 0:66], optv[:, :, 2 * T],
                                        ident32[0:66, 0:66])
                            return e.transpose(ps_tok[:, 128:194],
                                               optv[:, :, 2 * T + 1],
                                               ident32[0:66, 0:66])
                        em.op("tensor", toktr, reads=[opt_b[sl], ident32_b],
                              writes=[ps_tok_b])
                        def tokas(e, tsl=tsl):
                            e.tensor_copy(tok_sb[0:64, tsl, :],
                                          ps_tok[:, 0:66])
                            return e.tensor_copy(tok_sb[64:128, tsl, :],
                                                 ps_tok[:, 128:194])
                        em.op("vector", tokas, reads=[ps_tok_b],
                              writes=[tok_sb_b[tsl]])
                        ow_o = owh[:, sl, T, 0:64]
                        ow_nm = owh[:, sl, T, 64:65]
                        ow_l = owh[:, sl, T, 65:66]
                        em.op("vector",
                              lambda e, tsl=tsl, ow_nm=ow_nm: e.tensor_tensor(
                                  mg_nm[:, tsl, :], ow_nm,
                                  tok_sb[:, tsl, 64:65], ALU.min),
                              reads=[owh_b[sl], tok_sb_b[tsl]],
                              writes=[nm_b[tsl]])
                        def dcalc(e, tsl=tsl, ow_nm=ow_nm):
                            e.tensor_tensor(mg_d[:, tsl, 0:1],
                                            mg_nm[:, tsl, :], ow_nm,
                                            ALU.subtract)
                            return e.tensor_tensor(
                                mg_d[:, tsl, 1:2], mg_nm[:, tsl, :],
                                tok_sb[:, tsl, 64:65], ALU.subtract)
                        em.op("vector", dcalc,
                              reads=[nm_b[tsl], owh_b[sl], tok_sb_b[tsl]],
                              writes=[d_b[tsl]])
                        em.op("scalar",
                              lambda e, tsl=tsl: e.activation(
                                  mg_ab[:, tsl, :], mg_d[:, tsl, :], AF.Exp),
                              reads=[d_b[tsl]], writes=[ab_b[tsl]])
                        em.op("vector",
                              lambda e, tsl=tsl, ow_o=ow_o: e.tensor_scalar_mul(
                                  mg_t[:, tsl, :], ow_o, mg_ab[:, tsl, 0:1]),
                              reads=[owh_b[sl], ab_b[tsl]], writes=[t_b[tsl]])
                        em.op("vector",
                              lambda e, tsl=tsl: e.scalar_tensor_tensor(
                                  mg_o[:, tsl, :], tok_sb[:, tsl, 0:64],
                                  mg_ab[:, tsl, 1:2], mg_t[:, tsl, :],
                                  ALU.mult, ALU.add),
                              reads=[tok_sb_b[tsl], ab_b[tsl], t_b[tsl]],
                              writes=[o_b[tsl]])
                        def lprod(e, tsl=tsl, ow_l=ow_l):
                            e.tensor_tensor(mg_d[:, tsl, 0:1], ow_l,
                                            mg_ab[:, tsl, 0:1], ALU.mult)
                            return e.tensor_tensor(mg_d[:, tsl, 1:2],
                                                   tok_sb[:, tsl, 65:66],
                                                   mg_ab[:, tsl, 1:2],
                                                   ALU.mult)
                        em.op("vector", lprod,
                              reads=[owh_b[sl], tok_sb_b[tsl], ab_b[tsl]],
                              writes=[d_b[tsl]])
                        em.op("vector",
                              lambda e, tsl=tsl: e.reduce_sum(
                                  mg_l[:, tsl, :], mg_d[:, tsl, :], axis=AX.X),
                              reads=[d_b[tsl]], writes=[l_b[tsl]])
                        em.op("vector",
                              lambda e, tsl=tsl: e.reciprocal(
                                  rl2[:, tsl, :], mg_l[:, tsl, :]),
                              reads=[l_b[tsl]], writes=[rl2_b[tsl]])
                        em.op("vector",
                              lambda e, tsl=tsl: e.tensor_scalar_mul(
                                  attnb[:, tsl, :], mg_o[:, tsl, :],
                                  rl2[:, tsl, :]),
                              reads=[o_b[tsl], rl2_b[tsl]],
                              writes=[attnb_b[tsl]])
                        em.op("tensor",
                              lambda e, tsl=tsl: e.transpose(
                                  ps_aob[:, 0:128], attnb[:, tsl, :],
                                  ident[:]),
                              reads=[attnb_b[tsl], ident_b],
                              writes=[ps_aob_b])
                        base = (h % 2) * 64
                        em.op("vector",
                              lambda e, base=base, h=h, T=T: e.tensor_copy(
                                  aoT[base:base + 64, h // 2,
                                      T * 128:(T + 1) * 128],
                                  ps_aob[:, 0:128]),
                              reads=[ps_aob_b], writes=[aoT_b])
                if dbg:
                    em.op("scalar", lambda e: e.dma_start(dbg_ao_h, aoT[:]),
                          reads=[aoT_b], dma=True)
            em.run_phase("A2b")
            if upto == "a2b":
                return nc, em

            em.begin_phase()
            with contextlib.ExitStack() as ph:
                ec = ph.enter_context
                wo_sb = ec(nc.sbuf_tensor("wo_sb", [128, 8, D], BF16))
                x_t3 = ec(nc.sbuf_tensor("x_t3", [128, 2, D], F32))
                ln2 = ec(nc.sbuf_tensor("ln2", [128, D], F32))
                x2t = ec(nc.sbuf_tensor("x2t", [128, 2, D], F32))
                ln3 = ec(nc.sbuf_tensor("ln3", [128, D], BF16))
                ln3Ts = ec(nc.sbuf_tensor("ln3Ts", [128, 2, 8, 128], BF16))
                ps_pr = ec(nc.psum_tensor("ps_pr", [128, 2, 512], F32))
                ps_tr3 = ec(nc.psum_tensor("ps_tr3", [128, 2, 1024], BF16))
                wo_b = Buf("wo")
                x_t3_b = [Buf("x30"), Buf("x31")]
                ln2_b = Buf("ln2")
                x2t_b = [Buf("x20"), Buf("x21")]
                ln3_b = Buf("ln3")
                ln3Ts_b = [Buf("l3T0"), Buf("l3T1")]
                ps_pr_b = [Buf("ppr0"), Buf("ppr1")]
                ps_tr3_b = [Buf("ptr30"), Buf("ptr31")]

                em.op("gpsimd",
                      lambda e: e.dma_start(
                          wo_sb[:], wo_h.rearrange("(c p) o -> p c o", p=128)),
                      writes=[wo_b], dma=True)
                for T in range(NT):
                    sl = T % 2
                    c0 = T * 128
                    em.op("sync",
                          lambda e, sl=sl, c0=c0: e.dma_start(
                              x_t3[:, sl, :], x_h[c0:c0 + 128, :]),
                          writes=[x_t3_b[sl]], dma=True)
                    for oh in range(2):
                        psl = oh
                        def prmm(e, psl=psl, oh=oh, c0=c0):
                            for fg in range(8):
                                ins = e.matmul(
                                    ps_pr[:, psl, :], aoT[:, fg, c0:c0 + 128],
                                    wo_sb[:, fg, oh * 512:(oh + 1) * 512],
                                    start=(fg == 0), stop=(fg == 7))
                            return ins
                        em.op("tensor", prmm, reads=[wo_b, aoT_b],
                              writes=[ps_pr_b[psl]])
                        em.op("vector",
                              lambda e, psl=psl, oh=oh: e.tensor_copy(
                                  ln2[:, oh * 512:(oh + 1) * 512],
                                  ps_pr[:, psl, :]),
                              reads=[ps_pr_b[psl]], writes=[ln2_b])
                    rstd, negmr, rstd_b, negmr_b = ln_stats(ln2[:], [ln2_b])
                    em.op("scalar",
                          lambda e, r=rstd, m=negmr: e.activation(
                              ln2[:], ln2[:], AF.Identity, bias=m, scale=r),
                          reads=[ln2_b, rstd_b, negmr_b], writes=[ln2_b])
                    em.op("vector",
                          lambda e, sl=sl: e.tensor_add(
                              x2t[:, sl, :], x_t3[:, sl, :], ln2[:]),
                          reads=[x_t3_b[sl], ln2_b], writes=[x2t_b[sl]])
                    em.op("scalar",
                          lambda e, sl=sl, c0=c0: e.dma_start(
                              x2_h[c0:c0 + 128, :], x2t[:, sl, :]),
                          reads=[x2t_b[sl]], dma=True)
                    rstd, negmr, rstd_b, negmr_b = ln_stats(
                        x2t[:, sl, :], [x2t_b[sl]])
                    em.op("scalar",
                          lambda e, sl=sl, r=rstd, m=negmr: e.activation(
                              ln3[:], x2t[:, sl, :], AF.Identity, bias=m,
                              scale=r),
                          reads=[x2t_b[sl], rstd_b, negmr_b], writes=[ln3_b])
                    for f in range(8):
                        psl = f % 2
                        em.op("tensor",
                              lambda e, f=f, psl=psl: e.transpose(
                                  ps_tr3[:, psl, 0:128],
                                  ln3[:, f * 128:(f + 1) * 128], ident[:]),
                              reads=[ln3_b, ident_b], writes=[ps_tr3_b[psl]])
                        em.op("vector",
                              lambda e, f=f, sl=sl, psl=psl: e.tensor_copy(
                                  ln3Ts[:, sl, f, :], ps_tr3[:, psl, 0:128]),
                              reads=[ps_tr3_b[psl]], writes=[ln3Ts_b[sl]])
                    em.op("scalar",
                          lambda e, sl=sl, c0=c0: e.dma_start(
                              ln3T_hv[:, :, c0:c0 + 128], ln3Ts[:, sl, :, :]),
                          reads=[ln3Ts_b[sl]], dma=True)
            em.run_phase("A3")
            if upto == "a3":
                return nc, em

        em.begin_phase()
        with contextlib.ExitStack() as ph:
            ec = ph.enter_context
            s1w_c = ec(nc.sbuf_tensor("s1w_c", [128, 2, FH, 128], BF16))
            s2w_c = ec(nc.sbuf_tensor("s2w_c", [128, 2, FH, 4, 128], BF16))
            acc = ec(nc.sbuf_tensor("acc", [128, 16, D], F32))
            ln3c = ec(nc.sbuf_tensor("ln3c", [128, 2, 2048], BF16))
            wcp = ec(nc.sbuf_tensor("wcp", [128, 2, 16, D], BF16))
            ybf = ec(nc.sbuf_tensor("ybf", [128, 2, 4, 512], BF16))
            g_s = ec(nc.sbuf_tensor("g_s", [128, 2, 512], BF16))
            hT = ec(nc.sbuf_tensor("hT", [128, 2, 16, 512], BF16))
            x2b = ec(nc.sbuf_tensor("x2b", [128, 2, D], F32))
            lnf = ec(nc.sbuf_tensor("lnf", [128, D], F32))
            ps_o = ec(nc.psum_tensor("ps_o", [128, 2, 1024], F32))
            ps_y = ec(nc.psum_tensor("ps_y", [128, 512], F32))
            ps_z = ec(nc.psum_tensor("ps_z", [128, 2, 512], F32))

            s1w_b = [Buf("s1wa"), Buf("s1wb")]
            s2w_b = [Buf("s2wa"), Buf("s2wb")]
            acc_b = [Buf("acc%d" % i) for i in range(16)]
            ln3c_b = [Buf("l3c0"), Buf("l3c1")]
            wcp_b = [Buf("wcpa"), Buf("wcpb")]
            ybf_b = [Buf("ybf0"), Buf("ybf1")]
            g_s_b = [Buf("gs0"), Buf("gs1")]
            hT_b = [Buf("hT0"), Buf("hT1")]
            x2b_b = [Buf("x2b0"), Buf("x2b1")]
            lnf_b = Buf("lnf")
            ps_o_b = [Buf("pso0"), Buf("pso1")]
            ps_y_b = Buf("psy")
            ps_z_b = [Buf("psz0"), Buf("psz1")]

            wc_hv = wc_h.rearrange("(pc p) o -> pc p o", p=128)
            s1src = s1w_h.rearrange("h c (pp p) m -> c p h (pp m)", p=128)
            s2src = s2w_h.rearrange("h c q (pp p) m -> c h p q (pp m)", p=128)
            def chunk_loads(cidx):
                half_, c_ = cidx // NCHUNK, cidx % NCHUNK
                cset_ = cidx % 2
                t0h_ = half_ * 2048
                em.op("sync",
                      lambda e, c=c_, cset=cset_, t0h=t0h_: e.dma_start(
                          ln3c[:, cset, :], ln3T_hv[:, c, t0h:t0h + 2048]),
                      writes=[ln3c_b[cset_]], dma=True)
                em.op("gpsimd",
                      lambda e, c=c_, cset=cset_: e.dma_start(
                          s1w_c[:, cset, :, :], s1src[c]),
                      writes=[s1w_b[cset_]], dma=True)
                for fh in range(FH):
                    em.op("gpsimd",
                          lambda e, c=c_, cset=cset_, fh=fh: e.dma_start(
                              s2w_c[:, cset, fh, :, :], s2src[c, fh]),
                          writes=[s2w_b[cset_]], dma=True)
                for p in range(16):
                    em.op("gpsimd",
                          lambda e, c=c_, cset=cset_, p=p: e.dma_start(
                              wcp[:, cset, p, :], wc_hv[c * 16 + p]),
                          writes=[wcp_b[cset_]], dma=True)

            chunk_loads(0)
            pend_cpmm = None
            for half in range(2):
                t0h = half * 2048
                for c in range(NCHUNK):
                    cidx = half * NCHUNK + c
                    cset = cidx % 2
                    for st_ in ("tensor", "vector", "scalar"):
                        em.wait_only(st_, [("qG", em.cnt["qG"]),
                                           ("qY", em.cnt["qY"])])
                    if pend_cpmm is not None:
                        pend_cpmm()
                        pend_cpmm = None
                    if cidx + 1 < 16:
                        chunk_loads(cidx + 1)

                    for G in range(4):
                        gsl = G % 2
                        g0 = G * 512
                        for fh in range(4):
                            em.op("tensor",
                                  lambda e, cset=cset, g0=g0, fh=fh: e.matmul(
                                      ps_y[:], s1w_c[:, cset, fh, :],
                                      ln3c[:, cset, g0:g0 + 512],
                                      start=True, stop=True),
                                  reads=[s1w_b[cset], ln3c_b[cset]],
                                  writes=[ps_y_b])
                            em.op("vector",
                                  lambda e, gsl=gsl, fh=fh: e.tensor_copy(
                                      ybf[:, gsl, fh, :], ps_y[:]),
                                  reads=[ps_y_b], writes=[ybf_b[gsl]])
                        for fh in range(4):
                            for q in range(4):
                                piece = fh * 4 + q
                                zsl = piece % 2
                                em.op("tensor",
                                      lambda e, cset=cset, gsl=gsl, fh=fh, q=q,
                                      zsl=zsl: e.matmul(
                                          ps_z[:, zsl, :],
                                          s2w_c[:, cset, fh, q, :],
                                          ybf[:, gsl, fh, :],
                                          start=True, stop=True),
                                      reads=[s2w_b[cset], ybf_b[gsl]],
                                      writes=[ps_z_b[zsl]])
                                em.op("scalar",
                                      lambda e, zsl=zsl:
                                      e.activation(g_s[:, zsl, :],
                                                   ps_z[:, zsl, :], AF.Gelu),
                                      reads=[ps_z_b[zsl]],
                                      writes=[g_s_b[zsl]])
                                em.op("vector",
                                      lambda e, gsl=gsl, piece=piece, zsl=zsl:
                                      e.tensor_copy(hT[:, gsl, piece, :],
                                                    g_s[:, zsl, :]),
                                      reads=[g_s_b[zsl]], writes=[hT_b[gsl]])
                        def emit_cpmm(G=G, gsl=gsl, cset=cset, c=c):
                            for ti in range(4):
                                t = G * 4 + ti
                                tsl = t % 2
                                def cpmm(e, gsl=gsl, cset=cset, ti=ti,
                                         tsl=tsl):
                                    for p in range(16):
                                        for oh in range(2):
                                            ins = e.matmul(
                                                ps_o[:, tsl,
                                                     oh * 512:(oh + 1) * 512],
                                                hT[:, gsl, p,
                                                   ti * 128:(ti + 1) * 128],
                                                wcp[:, cset, p,
                                                    oh * 512:(oh + 1) * 512],
                                                start=(p == 0), stop=(p == 15))
                                    return ins
                                em.op("tensor", cpmm,
                                      reads=[hT_b[gsl], wcp_b[cset]],
                                      writes=[ps_o_b[tsl]])
                                if c == 0:
                                    em.op("vector",
                                          lambda e, tsl=tsl, t=t:
                                          e.tensor_copy(acc[:, t, :],
                                                        ps_o[:, tsl, :]),
                                          reads=[ps_o_b[tsl]],
                                          writes=[acc_b[t]])
                                else:
                                    em.op("vector",
                                          lambda e, tsl=tsl, t=t:
                                          e.tensor_tensor(
                                              acc[:, t, :], acc[:, t, :],
                                              ps_o[:, tsl, :], ALU.add),
                                          reads=[ps_o_b[tsl], acc_b[t]],
                                          writes=[acc_b[t]])
                        if pend_cpmm is not None:
                            pend_cpmm()
                        pend_cpmm = emit_cpmm
                if pend_cpmm is not None:
                    pend_cpmm()
                    pend_cpmm = None
                if dbg:
                    em.op("scalar",
                          lambda e, half=half: e.dma_start(
                              dbg_acc_h[half], acc[:]),
                          reads=list(acc_b), dma=True)
                for t in range(16):
                    tt = t0h + t * 128
                    sl = t % 2
                    em.op("sync",
                          lambda e, sl=sl, tt=tt: e.dma_start(
                              x2b[:, sl, :], x2_h[tt:tt + 128, :]),
                          writes=[x2b_b[sl]], dma=True)
                    rstd, negmr, rstd_b, negmr_b = ln_stats(
                        acc[:, t, :], [acc_b[t]])
                    em.op("scalar",
                          lambda e, t=t, r=rstd, m=negmr: e.activation(
                              lnf[:], acc[:, t, :], AF.Identity, bias=m,
                              scale=r),
                          reads=[acc_b[t], rstd_b, negmr_b], writes=[lnf_b])
                    em.op("vector",
                          lambda e, sl=sl: e.tensor_add(
                              x2b[:, sl, :], x2b[:, sl, :], lnf[:]),
                          reads=[x2b_b[sl], lnf_b], writes=[x2b_b[sl]])
                    em.op("scalar",
                          lambda e, sl=sl, tt=tt: e.dma_start(
                              y_h[tt:tt + 128, :], x2b[:, sl, :]),
                          reads=[x2b_b[sl]], dma=True)
        em.run_phase("B")
    return nc, em



def _build_masks():
    r = np.arange(128)[:, None]
    c = np.arange(192)[None, :]
    m_win = np.where((c >= r) & (c <= r + 64), 0.0, NEG).astype(np.float32)
    c0 = np.arange(128)[None, :]
    m_win0 = np.where((c0 <= r) & (c0 >= r - 64), 0.0, NEG).astype(np.float32)
    m_fix = np.full((NT, 128, 192), NEG, dtype=np.float32)
    diag = (c0 // 64 == r // 64) & (c0 <= r)
    m = np.arange(64)[None, :]
    for T in range(NT):
        m_fix[T, :, 0:128] = np.where(diag, 0.0, NEG)
        m_fix[T, :, 128:192] = np.where(m <= (2 * T + r // 64 - 1), 0.0, NEG)
    b = np.arange(64)[:, None]
    m_ph = np.where(m <= b - 2, 0.0, NEG).astype(np.float32)
    return m_win, m_win0, m_fix, m_ph


def _prep_inputs(L0_attn_x, L0_attn_Wq, L0_attn_Wk, L0_attn_Wv, L0_attn_Wo,
                 L0_ffn_rail_maker, L0_ffn_rail_up_proj, L0_ffn_w_cproj):
    x8 = np.ascontiguousarray(np.asarray(L0_attn_x, dtype=np.float32))
    wqT = np.ascontiguousarray(np.asarray(L0_attn_Wq, dtype=np.float32).T)
    wkT = np.ascontiguousarray(np.asarray(L0_attn_Wk, dtype=np.float32).T)
    wvT = np.ascontiguousarray(np.asarray(L0_attn_Wv, dtype=np.float32).T)
    woT = np.ascontiguousarray(np.asarray(L0_attn_Wo, dtype=np.float32).T)
    rm = np.asarray(L0_ffn_rail_maker, dtype=np.float32)
    rup = np.asarray(L0_ffn_rail_up_proj, dtype=np.float32)
    wc = np.asarray(L0_ffn_w_cproj, dtype=np.float32)

    s1w = np.zeros((FH, NCHUNK, 128, 128), np.float32)
    rmg = rm.reshape(FH, NCHUNK, 8, 16, 16)
    for rp in range(8):
        s1w[:, :, rp * 16:(rp + 1) * 16, rp * 16:(rp + 1) * 16] = rmg[:, :, rp]
    s2w = np.zeros((FH, NCHUNK, 4, 128, 128), np.float32)
    rupg = rup.reshape(FH, NCHUNK, 8, 4, 16, 16)
    for rp in range(8):
        s2w[:, :, :, rp * 16:(rp + 1) * 16, rp * 16:(rp + 1) * 16] = \
            rupg[:, :, rp].transpose(0, 1, 2, 4, 3)

    wct = np.ascontiguousarray(wc.T)
    R = np.arange(FF)
    cch = R // 2048
    rem = R % 2048
    piece = rem // 128
    mm_ = rem % 128
    hh = piece // 4
    qq = piece % 4
    rr = 8 * cch + mm_ // 16
    kk = 16 * qq + mm_ % 16
    wcT = np.ascontiguousarray(wct[hh * 4096 + rr * 64 + kk])

    m_win, m_win0, m_fix, m_ph = _build_masks()
    ident = np.eye(128, dtype=ml_dtypes.bfloat16)
    shared = dict(wqT=wqT, wkT=wkT, wvT=wvT, woT=woT, s1w=s1w, s2w=s2w,
                  wcT=wcT, m_win=m_win, m_win0=m_win0, m_fix=m_fix,
                  m_ph=m_ph,
                  m_ph2=np.ascontiguousarray(np.concatenate([m_ph, m_ph], 1)),
                  ident_c=ident)
    shared["m_win"] = np.ascontiguousarray(
        np.stack([m_win, m_win], axis=1))
    shared["m_win0"] = np.ascontiguousarray(
        np.stack([m_win0, m_win0], axis=1))
    shared["m_fix"] = np.ascontiguousarray(
        np.stack([m_fix, m_fix], axis=2))
    return x8, shared


def kernel(L0_attn_x, L0_attn_Wq, L0_attn_Wk, L0_attn_Wv, L0_attn_Wo,
           L0_ffn_rail_maker, L0_ffn_rail_up_proj, L0_ffn_w_cproj):
    x8, shared = _prep_inputs(L0_attn_x, L0_attn_Wq, L0_attn_Wk, L0_attn_Wv,
                              L0_attn_Wo, L0_ffn_rail_maker,
                              L0_ffn_rail_up_proj, L0_ffn_w_cproj)
    nc, _ = build_program("all")
    in_maps = [dict(shared, x=np.ascontiguousarray(x8[i])) for i in range(8)]
    res = run_bass_kernel_spmd(nc, in_maps, core_ids=list(range(8)))
    out = np.stack([res.results[i]["y"] for i in range(8)], axis=0)
    return out.astype(np.float32)
